# version 13
# baseline (speedup 1.0000x reference)
import numpy as np
import ml_dtypes
from contextlib import ExitStack
import concourse.bass as bass
import concourse.mybir as mybir
from concourse.bass_utils import run_bass_kernel_spmd

F32 = mybir.dt.float32
BF16 = mybir.dt.bfloat16
AF = mybir.ActivationFunctionType
OP = mybir.AluOpType
AX = mybir.AxisListType
bfdt = ml_dtypes.bfloat16

B, L, DM = 2, 4096, 2048
DI, NST, RK, CW = 4096, 16, 128, 4
DH = 8192
DCH = 8
TQ = 1024
SEG = 1024
NSEG = L // SEG
MAGIC = float(np.float32(1.5 * 2.0 ** 23))
EPS_Q = 1e-5
EPS_LN = 1e-5
ENGINES = ("sync", "vector", "scalar", "tensor", "gpsimd")


class Sem:
    def __init__(self, stack, nc, name):
        self.h = stack.enter_context(nc.semaphore(name))
        self.n = 0


class Prog:

    def __init__(self):
        self.streams = {e: [] for e in ENGINES}

    def wait(self, eng, sem, val=None):
        v = sem.n if val is None else val
        if v <= 0:
            return
        self.streams[eng].append(lambda E, h=sem.h, v=v: E.wait_ge(h, v))

    def run(self, eng, fn, inc=None, inck=1):
        if inc is not None:
            inc.n += inck
            self.streams[eng].append(
                lambda E, fn=fn, h=inc.h, k=inck: fn(E).then_inc(h, k))
        else:
            self.streams[eng].append(fn)


def build_nc():
    nc = bass.Bass("TRN2", target_bir_lowering=False)
    P = Prog()

    def din(name, shape, dt=F32):
        return nc.dram_tensor(name, shape, dt, kind="ExternalInput").ap()

    hidT = din("hidT", [DM, L])
    hidQ = din("hidQ", [TQ, DM])
    wInT = din("wInT", [DM, 2048])
    convW = din("convW", [128, DCH * CW])
    convB = din("convB", [128, DCH])
    negA_i = din("negA_i", [128, DCH * NST])
    D_i = din("D_i", [128, DCH])
    dtb_i = din("dtb_i", [128, DCH])
    xpT = din("xpT", [128, DCH * 160])
    dtwT = din("dtwT", [128, 1024])
    woT = din("woT", [128, DCH * DM])
    wnorm_r = din("wnorm_r", [1, DM])
    ones_r = din("ones_r", [1, 128])
    ones_c = din("ones_c", [128, 1])
    oh_bf = din("oh_bf", [32, 32 * 128], BF16)
    wgvT_s = din("wgvT_s", [256, DH * 2])
    woutT_s = din("woutT_s", [1024, DM])

    out_q = nc.dram_tensor("out_q", [TQ, DM], F32, kind="ExternalOutput").ap()

    z_sp = nc.dram_tensor("z_sp", [128, DCH * L], BF16).ap()
    u_sp = nc.dram_tensor("u_sp", [128, DCH * L], F32).ap()
    xdbl_i = nc.dram_tensor("xdbl_i", [160, L], F32).ap()
    xdbl_o = nc.dram_tensor("xdbl_o", [160, L], F32).ap()
    wst_i = nc.dram_tensor("wst_i", [1, 16], F32).ap()
    wst_o = nc.dram_tensor("wst_o", [1, 16], F32).ap()
    gvq_i = nc.dram_tensor("gvq_i", [256, DH * 2], BF16).ap()
    gvq_o = nc.dram_tensor("gvq_o", [DM, DH * 2], BF16, addr_space="Shared").ap()
    woq_i = nc.dram_tensor("woq_i", [1024, DM], BF16).ap()
    woq_o = nc.dram_tensor("woq_o", [DH, DM], BF16, addr_space="Shared").ap()
    rs_i = nc.dram_tensor("rs_i", [L, DM], F32).ap()
    rs_o = nc.dram_tensor("rs_o", [TQ, DM], F32).ap()
    hq_tm_sp = nc.dram_tensor("hq_tm_sp", [TQ, DM], BF16).ap()
    uffn_sp = nc.dram_tensor("uffn_sp", [128, 8 * DH], F32).ap()
    uq_tm_sp = nc.dram_tensor("uq_tm_sp", [TQ, DH], BF16).ap()

    g = ExitStack()
    sb_ = lambda st, n, s, d=F32: st.enter_context(nc.sbuf_tensor(n, s, d))

    with g:
        ld = Sem(g, nc, "ld"); gd = Sem(g, nc, "gd")
        pe = Sem(g, nc, "pe"); ve = Sem(g, nc, "ve"); ac = Sem(g, nc, "ac")
        c_xd = Sem(g, nc, "c_xd"); c_ws = Sem(g, nc, "c_ws")
        c_gv = Sem(g, nc, "c_gv"); c_wo = Sem(g, nc, "c_wo")
        c_rs = Sem(g, nc, "c_rs"); so = Sem(g, nc, "so")
        po = Sem(g, nc, "po")

        psum = [g.enter_context(nc.psum_tensor(f"ps{i}", [128, 512], F32))
                for i in range(8)]
        ps_state = [None] * 8
        ps_rr = [0]

        def ps_get():
            i = ps_rr[0]; ps_rr[0] = (ps_rr[0] + 1) % 8
            return i, psum[i]

        def ps_wait(eng, bi):
            if ps_state[bi]:
                P.wait(eng, *ps_state[bi])

        const_ones = sb_(g, "const_ones", [1, 128])
        const_onec = sb_(g, "const_onec", [128, 1])
        const_oh = sb_(g, "const_oh", [32, 32 * 128], BF16)
        s_convW = sb_(g, "s_convW", [128, DCH * CW])
        s_convB = sb_(g, "s_convB", [128, DCH])
        s_negA = sb_(g, "s_negA", [128, DCH * NST])
        s_D = sb_(g, "s_D", [128, DCH])
        s_dtb = sb_(g, "s_dtb", [128, DCH])
        stat = sb_(g, "stat", [128, 32])
        carry = sb_(g, "carry", [128, 128])
        red1 = sb_(g, "red1", [128, 1])
        gv_rsum = sb_(g, "gv_rsum", [128, 2])
        wst_sb = sb_(g, "wst_sb", [1, 16])
        sc_sb = sb_(g, "sc_sb", [128, 8])
        shinv = sb_(g, "shinv", [128, 8])
        sgv = sb_(g, "sgv", [128, 8])
        umax = sb_(g, "umax", [128, 8])
        suinv = sb_(g, "suinv", [128, 8])
        red_all = sb_(g, "red_all", [128, 128])

        def dma(eng, sem, out, in_):
            P.run(eng, lambda E, o=out, i=in_: E.dma_start(out=o, in_=i),
                  inc=sem, inck=16)

        def dma_t(eng, sem, out, in_):
            P.run(eng, lambda E, o=out, i=in_: E.dma_start(out=o, in_=i,
                                                           transpose=True),
                  inc=sem, inck=16)

        dma("sync", ld, const_ones[:], ones_r)
        dma("sync", ld, const_onec[:], ones_c)
        dma("sync", ld, const_oh[:], oh_bf)
        dma("sync", ld, s_convW[:], convW)
        dma("sync", ld, s_convB[:], convB)
        dma("sync", ld, s_negA[:], negA_i)
        dma("sync", ld, s_D[:], D_i)
        dma("sync", ld, s_dtb[:], dtb_i)
        ld_consts = ld.n

        sA = ExitStack()
        x_bf = sb_(sA, "x_bf", [128, DCH * (L + 8)], BF16)
        sA1 = ExitStack()
        wIn_bf = sb_(sA1, "wIn_bf", [128, 16 * 2048], BF16)
        wtmp = sb_(sA1, "wtmp", [128, 1024])
        hid_f = sb_(sA1, "hid_f", [128, 16 * 256])
        hid_b = [sb_(sA1, f"hid_b{i}", [128, 16 * 256], BF16) for i in range(2)]
        zstage = sb_(sA1, "zstage", [128, DCH * 256], BF16)

        for d in range(DCH):
            P.run("vector", lambda E, d=d: E.memset(
                x_bf[:, d * (L + 8):d * (L + 8) + 8], 0.0), inc=ve)
        wcast = []
        for k in range(32):
            if k >= 1:
                P.wait("sync", ve, wcast[k - 1])
            dma("sync", ld, wtmp[:], wInT.rearrange(
                "(k p) e -> k p e", p=128)[k // 2][
                :, (k % 2) * 1024:(k % 2 + 1) * 1024])
            P.wait("vector", ld)
            P.run("vector", lambda E, k=k: E.tensor_copy(
                wIn_bf[:, k * 1024:(k + 1) * 1024], wtmp[:]), inc=ve)
            wcast.append(ve.n)
        ve_win = ve.n

        hidb_state = [None, None]
        hidf_state = None
        z_dma = 0
        for tc in range(16):
            bsel = tc % 2
            if hidf_state:
                P.wait("sync", *hidf_state)
            for kq in range(4):
                dma("sync", ld, hid_f[:, kq * 4 * 256:(kq + 1) * 4 * 256],
                    hidT.rearrange("(k p) t -> p k t", p=128)[
                        :, kq * 4:(kq + 1) * 4, tc * 256:(tc + 1) * 256])
            P.wait("vector", ld)
            if hidb_state[bsel]:
                P.wait("vector", *hidb_state[bsel])
            P.run("vector", lambda E, b=bsel: E.tensor_copy(
                hid_b[b][:], hid_f[:]), inc=ve)
            hidf_state = (ve, ve.n)
            cast_cnt = ve.n
            for m in range(16):
                bi, pj = ps_get()
                pv = pj[:, 0:256]
                P.wait("tensor", ve, max(cast_cnt, ve_win))
                ps_wait("tensor", bi)
                for k in range(16):
                    P.run("tensor", lambda E, pv=pv, m=m, k=k, b=bsel: E.matmul(
                        pv, wIn_bf[:, k * 2048 + m * 128:k * 2048 + (m + 1) * 128],
                        hid_b[b][:, k * 256:(k + 1) * 256],
                        start=(k == 0), stop=(k == 15)),
                        inc=(pe if k == 15 else None))
                P.wait("scalar", pe)
                if m < 8:
                    P.run("scalar", lambda E, pv=pv, m=m, tc=tc: E.activation(
                        x_bf[:, m * (L + 8) + 8 + tc * 256:
                             m * (L + 8) + 8 + (tc + 1) * 256], pv, AF.Copy),
                        inc=ac)
                else:
                    if m == 8 and z_dma:
                        P.wait("scalar", ld, z_dma)
                    P.run("scalar", lambda E, pv=pv, m=m: E.activation(
                        zstage[:, (m - 8) * 256:(m - 7) * 256], pv, AF.Copy),
                        inc=ac)
                ps_state[bi] = (ac, ac.n)
            hidb_state[bsel] = (pe, pe.n)
            P.wait("sync", ac)
            for zc in range(8):
                dma("sync", ld, z_sp[:, zc * L + tc * 256:zc * L + (tc + 1) * 256],
                    zstage[:, zc * 256:(zc + 1) * 256])
            z_dma = ld.n
        ac_inproj = ac.n
        sA1.close()

        sWq = ExitStack()
        wq_f = sb_(sWq, "wq_f", [128, 1024])
        wq_g = sb_(sWq, "wq_g", [128, 1024])
        wq_b = sb_(sWq, "wq_b", [128, 1024], BF16)
        P.wait("gpsimd", pe)
        P.wait("gpsimd", ld, z_dma)
        qsrc = []
        for t in range(2):
            for q in range(16):
                qsrc.append((wgvT_s.rearrange("(t p) o -> t p o", p=128)[t][
                    :, q * 1024:(q + 1) * 1024],
                    gvq_i.rearrange("(t p) o -> t p o", p=128)[t][
                    :, q * 1024:(q + 1) * 1024], 1024, 0))
        for t in range(8):
            for q in range(2):
                qsrc.append((woutT_s.rearrange("(t p) o -> t p o", p=128)[t][
                    :, q * 1024:(q + 1) * 1024],
                    woq_i.rearrange("(t p) o -> t p o", p=128)[t][
                    :, q * 1024:(q + 1) * 1024], 1024, 1))
        last_red = None
        for i, (srcp, dstp, w, which) in enumerate(qsrc):
            if last_red is not None:
                P.wait("gpsimd", ac, last_red)
            dma("gpsimd", gd, wq_f[:, 0:w], srcp)
            P.wait("scalar", gd)
            P.run("scalar", lambda E, i=i, w=w: E.activation(
                wq_g[:, 0:w], wq_f[:, 0:w], AF.Abs,
                accum_out=red_all[:, i:i + 1]), inc=ac)
            last_red = ac.n
        P.wait("scalar", ac, last_red)
        P.run("scalar", lambda E: E.activation(
            red_all[:, 48:96], red_all[:, 0:48], AF.Copy), inc=ac)
        P.wait("vector", ac)
        P.run("vector", lambda E: E.tensor_reduce(
            gv_rsum[:, 0:1], red_all[:, 48:80], AX.X, OP.add), inc=ve)
        P.run("vector", lambda E: E.tensor_reduce(
            gv_rsum[:, 1:2], red_all[:, 80:96], AX.X, OP.add), inc=ve)
        last_red = ve.n
        bi, pj = ps_get()
        ps_wait("tensor", bi)
        P.wait("tensor", ve, last_red)
        P.wait("tensor", ld, ld_consts)
        P.run("tensor", lambda E, pj=pj: E.matmul(
            pj[0:1, 0:2], const_onec[:], gv_rsum[:], start=True, stop=True),
            inc=pe)
        P.wait("scalar", pe)
        P.run("scalar", lambda E, pj=pj: E.activation(
            wst_sb[0:1, 0:2], pj[0:1, 0:2], AF.Copy), inc=ac)
        ps_state[bi] = (ac, ac.n)
        P.wait("gpsimd", ac)
        dma("gpsimd", gd, wst_i, wst_sb[:])
        P.wait("gpsimd", gd)
        P.run("gpsimd", lambda E: E.collective_compute(
            "AllReduce", OP.add, replica_groups=[[0, 1, 2, 3, 4, 5, 6, 7]],
            ins=[wst_i], outs=[wst_o]), inc=c_ws)
        gd_q1 = gd.n
        ve_q1 = ve.n

        sA2 = ExitStack()
        cacc = sb_(sA2, "cacc", [128, L])
        u_f32 = sb_(sA2, "u_f32", [128, L])
        u_bf = sb_(sA2, "u_bf", [128, DCH * L], BF16)
        xp_bf = sb_(sA2, "xp_bf", [128, DCH * 160], BF16)
        xdblB = sb_(sA2, "xdblB", [32, L])
        u_dma_prev = 0
        for d in range(DCH):
            base = d * (L + 8) + 8
            P.wait("vector", ac, ac_inproj)
            P.run("vector", lambda E, d=d, base=base: E.tensor_scalar(
                cacc[:], x_bf[:, base - 3:base - 3 + L],
                s_convW[:, d * CW:d * CW + 1], None, OP.mult), inc=ve)
            for j in range(1, 4):
                P.run("vector", lambda E, d=d, base=base, j=j: E.scalar_tensor_tensor(
                    cacc[:], x_bf[:, base - 3 + j:base - 3 + j + L],
                    s_convW[:, d * CW + j:d * CW + j + 1], cacc[:],
                    OP.mult, OP.add), inc=ve)
            P.wait("scalar", ve)
            if u_dma_prev:
                P.wait("scalar", ld, u_dma_prev)
            P.run("scalar", lambda E, d=d: E.activation(
                u_f32[:], cacc[:], AF.Silu, bias=s_convB[:, d:d + 1]), inc=ac)
            P.wait("vector", ac)
            P.run("vector", lambda E, d=d: E.tensor_copy(
                u_bf[:, d * L:(d + 1) * L], u_f32[:]), inc=ve)
            P.wait("sync", ac)
            dma("sync", ld, u_sp[:, d * L:(d + 1) * L], u_f32[:])
            u_dma_prev = ld.n
        ve_conv = ve.n
        ld_usp = ld.n

        dma("sync", ld, cacc[:, 0:DCH * 160], xpT)
        P.wait("vector", ld)
        P.wait("vector", ve, ve_conv)
        P.run("vector", lambda E: E.tensor_copy(xp_bf[:], cacc[:, 0:DCH * 160]),
              inc=ve)
        ve_xp = ve.n
        xdblA = cacc
        for tc in range(8):
            bi, pj = ps_get()
            bi2, pj2 = ps_get()
            P.wait("tensor", ve, ve_xp)
            ps_wait("tensor", bi)
            ps_wait("tensor", bi2)
            for k in range(DCH):
                P.run("tensor", lambda E, pj=pj, k=k, tc=tc: E.matmul(
                    pj[:, 0:512], xp_bf[:, k * 160:k * 160 + 128],
                    u_bf[:, k * L + tc * 512:k * L + (tc + 1) * 512],
                    start=(k == 0), stop=(k == DCH - 1)),
                    inc=(pe if k == DCH - 1 else None))
                P.run("tensor", lambda E, pj2=pj2, k=k, tc=tc: E.matmul(
                    pj2[0:32, 0:512], xp_bf[:, k * 160 + 128:(k + 1) * 160],
                    u_bf[:, k * L + tc * 512:k * L + (tc + 1) * 512],
                    start=(k == 0), stop=(k == DCH - 1)),
                    inc=(pe if k == DCH - 1 else None))
            P.wait("scalar", pe)
            P.run("scalar", lambda E, pj=pj, tc=tc: E.activation(
                xdblA[:, tc * 512:(tc + 1) * 512], pj[:, 0:512], AF.Copy), inc=ac)
            P.run("scalar", lambda E, pj2=pj2, tc=tc: E.activation(
                xdblB[:, tc * 512:(tc + 1) * 512], pj2[0:32, 0:512], AF.Copy),
                inc=ac)
            ps_state[bi] = (ac, ac.n)
            ps_state[bi2] = (ac, ac.n)
        P.wait("gpsimd", ac)
        dma("gpsimd", gd, xdbl_i[0:128, :], xdblA[:])
        dma("gpsimd", gd, xdbl_i[128:160, :], xdblB[:])
        P.wait("gpsimd", gd)
        P.run("gpsimd", lambda E: E.collective_compute(
            "AllReduce", OP.add, replica_groups=[[0, 1, 2, 3], [4, 5, 6, 7]],
            ins=[xdbl_i], outs=[xdbl_o]), inc=c_xd)

        P.wait("gpsimd", c_ws)
        dma("gpsimd", gd, wst_sb[:], wst_o)
        P.wait("vector", gd)
        P.run("vector", lambda E: E.tensor_scalar(
            wst_sb[0:1, 2:3], wst_sb[0:1, 0:1], 1.0 / (16384.0 * 2048.0), EPS_Q,
            OP.mult, OP.max), inc=ve)
        P.run("vector", lambda E: E.tensor_scalar(
            wst_sb[0:1, 3:4], wst_sb[0:1, 1:2], 1.0 / (8192.0 * 2048.0), EPS_Q,
            OP.mult, OP.max), inc=ve)
        P.wait("scalar", ve)
        P.run("scalar", lambda E: E.activation(
            wst_sb[0:1, 6:8], wst_sb[0:1, 2:4], AF.Copy), inc=ac)
        P.wait("vector", ac)
        P.run("vector", lambda E: E.reciprocal(
            wst_sb[0:1, 4:6], wst_sb[0:1, 6:8]), inc=ve)
        ve_wsc = ve.n
        bi, pj = ps_get()
        ps_wait("tensor", bi)
        P.wait("tensor", ve, ve_wsc)
        P.run("tensor", lambda E, pj=pj: E.matmul(
            pj[:, 0:4], const_ones[0:1, :], wst_sb[0:1, 2:6],
            start=True, stop=True), inc=pe)
        P.wait("scalar", pe)
        P.run("scalar", lambda E, pj=pj: E.activation(
            sc_sb[:, 0:4], pj[:, 0:4], AF.Copy), inc=ac)
        ps_state[bi] = (ac, ac.n)
        ac_wsc = ac.n
        P.wait("vector", ac, ac_wsc)
        P.run("vector", lambda E: E.tensor_single_scalar(
            sc_sb[:, 5:6], sc_sb[:, 0:1], 1.0 / 127.0, OP.mult), inc=ve)
        qbuf_state = [(ac, ac.n), (ac, ac.n)]
        for i, (srcp, dstp, w, which) in enumerate(qsrc):
            bsl = i % 2
            scl = sc_sb[:, 2:3] if which == 0 else sc_sb[:, 3:4]
            wf = wq_f[:, bsl * 512:bsl * 512 + 512] if False else None
            if qbuf_state[bsl]:
                P.wait("sync", *qbuf_state[bsl])
            src_f = wq_f if bsl == 0 else wq_g
            dma("sync", ld, src_f[:, 0:w], srcp)
            P.wait("gpsimd", ld)
            P.wait("gpsimd", ac, ac_wsc)
            P.run("gpsimd", lambda E, w=w, scl=scl, src_f=src_f: E.tensor_scalar(
                src_f[:, 0:w], src_f[:, 0:w], scl, MAGIC, OP.mult, OP.add),
                inc=po)
            P.run("gpsimd", lambda E, w=w, src_f=src_f: E.tensor_scalar(
                src_f[:, 0:w], src_f[:, 0:w], -MAGIC, None, OP.add), inc=po)
            P.run("gpsimd", lambda E, w=w, src_f=src_f: E.tensor_scalar(
                wq_b[:, 0:w], src_f[:, 0:w], -1.0, 1.0, OP.max, OP.min), inc=po)
            qbuf_state[bsl] = (po, po.n)
            P.wait("sync", po)
            dma("sync", ld, dstp, wq_b[:, 0:w])
            P.wait("gpsimd", ld)
        ld_qw = ld.n
        P.wait("gpsimd", ld, ld_qw)
        P.run("gpsimd", lambda E: E.collective_compute(
            "AllGather", OP.bypass, replica_groups=[[0, 1, 2, 3, 4, 5, 6, 7]],
            ins=[gvq_i], outs=[gvq_o]), inc=c_gv)
        P.run("gpsimd", lambda E: E.collective_compute(
            "AllGather", OP.bypass, replica_groups=[[0, 1, 2, 3, 4, 5, 6, 7]],
            ins=[woq_i], outs=[woq_o]), inc=c_wo)
        gd_wq = gd.n
        ve_wq = ve.n
        ac_wq = ac.n

        sA2.close()
        sWq.close()
        sA.close()


        sB = ExitStack()
        y_bf = sb_(sB, "y_bf", [128, DCH * L], BF16)
        sB1 = ExitStack()
        dt_bf = sb_(sB1, "dt_bf", [128, L], BF16)
        BC_bf = sb_(sB1, "BC_bf", [32, L], BF16)
        dtw_bf = sb_(sB1, "dtw_bf", [128, 1024], BF16)
        delta_all = sb_(sB1, "delta_all", [128, DCH * SEG], BF16)
        du_all = sb_(sB1, "du_all", [128, DCH * SEG], BF16)
        y_acc = sb_(sB1, "y_acc", [128, DCH * SEG])
        Bb_sb = [sb_(sB1, f"Bb{i}", [128, SEG], BF16) for i in range(2)]
        Cb_sb = [sb_(sB1, f"Cb{i}", [128, SEG], BF16) for i in range(2)]
        a_t = [sb_(sB1, f"a_t{i}", [128, SEG]) for i in range(2)]
        b_t = [sb_(sB1, f"b_t{i}", [128, SEG]) for i in range(2)]
        s_t2 = [sb_(sB1, f"s_t2{i}", [128, SEG]) for i in range(2)]
        prod_bf = sb_(sB1, "prod_bf", [128, SEG], BF16)
        useg = sb_(sB1, "useg", [128, SEG])
        zseg = sb_(sB1, "zseg", [128, SEG], BF16)

        P.wait("sync", gd, gd_wq)
        P.wait("sync", c_xd)
        dma("sync", ld, y_acc[:, 0:L],
            xdbl_o[0:128, :])
        dma("sync", ld, y_acc[0:32, SEG * 4:SEG * 4 + L], xdbl_o[128:160, :])
        ld_xdbl = ld.n
        P.wait("vector", ld, ld_xdbl)
        P.wait("vector", ve, ve_wq)
        P.run("vector", lambda E: E.tensor_copy(
            dt_bf[:], y_acc[:, 0:L]), inc=ve)
        P.run("vector", lambda E: E.tensor_copy(
            BC_bf[:], y_acc[0:32, SEG * 4:SEG * 4 + L]), inc=ve)
        dma("sync", ld, useg[:], dtwT)
        P.wait("vector", ld)
        P.run("vector", lambda E: E.tensor_copy(dtw_bf[:], useg[:, 0:1024]),
              inc=ve)
        ve_dtprep = ve.n
        fin_state = (ve, ve_dtprep)

        ab_state = [None, None]
        bc_state = [None, None]
        P.run("vector", lambda E: E.memset(stat[:, 31:32], 0.0), inc=ve)
        for sg in range(NSEG):
            t0 = sg * SEG
            P.run("vector", lambda E: E.memset(y_acc[:], 0.0), inc=ve)
            for d in range(DCH):
                sp0, sp1 = a_t[0], b_t[0]
                bi, pj = ps_get()
                bi2, pj2 = ps_get()
                P.wait("tensor", ve, ve_dtprep)
                if ab_state[0]:
                    P.wait("tensor", *ab_state[0])
                ps_wait("tensor", bi)
                ps_wait("tensor", bi2)
                for half, px in ((0, pj), (1, pj2)):
                    P.run("tensor", lambda E, px=px, d=d, half=half, t0=t0: E.matmul(
                        px[:, 0:512], dtw_bf[:, d * 128:(d + 1) * 128],
                        dt_bf[:, t0 + half * 512:t0 + (half + 1) * 512],
                        start=True, stop=True), inc=pe)
                P.wait("vector", pe)
                P.wait("vector", ac, ac_wq)
                if ab_state[0]:
                    P.wait("vector", *ab_state[0])
                for half, px in ((0, pj), (1, pj2)):
                    P.run("vector", lambda E, px=px, d=d, half=half, sp0=sp0:
                          E.tensor_scalar(
                        sp0[:, half * 512:(half + 1) * 512], px[:, 0:512],
                        s_dtb[:, d:d + 1], None, OP.add), inc=ve)
                ps_state[bi] = (ve, ve.n)
                ps_state[bi2] = (ve, ve.n)
                P.wait("scalar", ve)
                P.run("scalar", lambda E, sp0=sp0, sp1=sp1: E.activation(
                    sp1[:], sp0[:], AF.Abs), inc=ac)
                P.run("scalar", lambda E, sp1=sp1: E.activation(
                    sp1[:], sp1[:], AF.Exp, scale=-1.0), inc=ac)
                P.run("scalar", lambda E, sp1=sp1: E.activation(
                    sp1[:], sp1[:], AF.Ln, bias=1.0), inc=ac)
                P.wait("vector", ac)
                P.run("vector", lambda E, d=d, sp0=sp0, sp1=sp1: E.scalar_tensor_tensor(
                    delta_all[:, d * SEG:(d + 1) * SEG], sp0[:], 0.0, sp1[:],
                    OP.max, OP.add), inc=ve)
                if fin_state:
                    P.wait("sync", *fin_state)
                P.wait("sync", ld, ld_usp)
                dma("sync", ld, useg[:], u_sp[:, d * L + t0:d * L + t0 + SEG])
                P.wait("vector", ld)
                P.run("vector", lambda E, d=d: E.tensor_mul(
                    du_all[:, d * SEG:(d + 1) * SEG],
                    delta_all[:, d * SEG:(d + 1) * SEG], useg[:]), inc=ve)
                fin_state = (ve, ve.n)
                ab_state = [None, None]
            ve_du = ve.n

            for n in range(NST):
                par = n % 2
                for which, dst in ((0, Bb_sb[par]), (1, Cb_sb[par])):
                    ohn = n if which == 0 else 16 + n
                    for h2 in range(2):
                        bi, pj = ps_get()
                        P.wait("tensor", ve, ve_dtprep)
                        if bc_state[par]:
                            P.wait("tensor", *bc_state[par])
                        ps_wait("tensor", bi)
                        P.run("tensor", lambda E, pj=pj, ohn=ohn, h2=h2, t0=t0:
                              E.matmul(
                            pj[:, 0:512], const_oh[:, ohn * 128:(ohn + 1) * 128],
                            BC_bf[:, t0 + h2 * 512:t0 + (h2 + 1) * 512],
                            start=True, stop=True), inc=pe)
                        P.wait("scalar", pe)
                        P.run("scalar", lambda E, pj=pj, dst=dst, h2=h2:
                              E.activation(
                            dst[:, h2 * 512:(h2 + 1) * 512], pj[:, 0:512],
                            AF.Copy), inc=ac)
                        ps_state[bi] = (ac, ac.n)
                ac_bc = ac.n
                for d in range(DCH):
                    abp = (n * DCH + d) % 2
                    at, bt = a_t[abp], b_t[abp]
                    if ab_state[abp]:
                        P.wait("scalar", *ab_state[abp])
                    P.wait("scalar", ve, ve_du)
                    P.run("scalar", lambda E, d=d, n=n, at=at: E.activation(
                        at[:], delta_all[:, d * SEG:(d + 1) * SEG], AF.Exp,
                        scale=s_negA[:, d * NST + n:d * NST + n + 1]), inc=ac)
                    if (n * DCH + d) % 4 == 0:
                        P.wait("vector", ac)
                        P.run("vector", lambda E, d=d, bt=bt, par=par: E.tensor_mul(
                            bt[:], du_all[:, d * SEG:(d + 1) * SEG], Bb_sb[par][:]),
                            inc=ve)
                    else:
                        P.wait("gpsimd", ac, ac_bc)
                        P.wait("gpsimd", ve, ve_du)
                        if ab_state[abp]:
                            P.wait("gpsimd", *ab_state[abp])
                        P.run("gpsimd", lambda E, d=d, bt=bt, par=par: E.tensor_mul(
                            bt[:], du_all[:, d * SEG:(d + 1) * SEG], Bb_sb[par][:]),
                            inc=po)
                        P.wait("vector", po)
                    cidx = d * NST + n
                    stt = s_t2[abp]
                    if sg == 0:
                        P.run("vector", lambda E, at=at, bt=bt, stt=stt:
                              E.tensor_tensor_scan(
                            stt[:], at[:], bt[:], 0.0, OP.mult, OP.add), inc=ve)
                    else:
                        P.run("vector", lambda E, at=at, bt=bt, cidx=cidx, stt=stt:
                              E.tensor_tensor_scan(
                            stt[:], at[:], bt[:], carry[:, cidx:cidx + 1],
                            OP.mult, OP.add), inc=ve)
                    ab_state[abp] = (ve, ve.n)
                    P.run("vector", lambda E, par=par, stt=stt: E.tensor_mul(
                        prod_bf[:], stt[:], Cb_sb[par][:]), inc=ve)
                    P.run("vector", lambda E, d=d: E.tensor_add(
                        y_acc[:, d * SEG:(d + 1) * SEG],
                        y_acc[:, d * SEG:(d + 1) * SEG], prod_bf[:]), inc=ve)
                    if sg < NSEG - 1:
                        P.run("vector", lambda E, cidx=cidx, stt=stt: E.tensor_add(
                            carry[:, cidx:cidx + 1], stt[:, SEG - 1:SEG],
                            stat[:, 31:32]), inc=ve)
                bc_state[par] = (ve, ve.n)
            for d in range(DCH):
                if fin_state:
                    P.wait("sync", *fin_state)
                dma("sync", ld, useg[:], u_sp[:, d * L + t0:d * L + t0 + SEG])
                dma("sync", ld, zseg[:], z_sp[:, d * L + t0:d * L + t0 + SEG])
                P.wait("scalar", ld)
                P.wait("scalar", ve)
                gseg = a_t[0]
                P.run("scalar", lambda E, gseg=gseg: E.activation(
                    gseg[:], zseg[:], AF.Silu), inc=ac)
                P.wait("vector", ld)
                P.wait("vector", ac)
                P.run("vector", lambda E, d=d: E.scalar_tensor_tensor(
                    useg[:], useg[:], s_D[:, d:d + 1],
                    y_acc[:, d * SEG:(d + 1) * SEG], OP.mult, OP.add), inc=ve)
                P.run("vector", lambda E, d=d, t0=t0, gseg=gseg: E.tensor_mul(
                    y_bf[:, d * L + t0:d * L + t0 + SEG], useg[:], gseg[:]),
                    inc=ve)
                fin_state = (ve, ve.n)
                ab_state[0] = (ve, ve.n)
        ve_y = ve.n
        sB1.close()

        sB2 = ExitStack()
        wo_bf = sb_(sB2, "wo_bf", [128, DCH * DM], BF16)
        ostage = sb_(sB2, "ostage", [128, DM])
        wtmp2 = sb_(sB2, "wtmp2", [128, DM])
        wo_cast = 0
        for d in range(DCH):
            if wo_cast:
                P.wait("sync", ve, wo_cast)
            dma("sync", ld, wtmp2[:], woT.rearrange("p (d m) -> d p m", d=DCH)[d])
            P.wait("vector", ld)
            P.run("vector", lambda E, d=d: E.tensor_copy(
                wo_bf[:, d * DM:(d + 1) * DM], wtmp2[:]), inc=ve)
            wo_cast = ve.n
        ve_wo = ve.n
        ost_dma = 0
        for tc in range(32):
            for mq in range(4):
                bi, pj = ps_get()
                P.wait("tensor", ve, max(ve_y, ve_wo))
                ps_wait("tensor", bi)
                for d in range(DCH):
                    P.run("tensor", lambda E, pj=pj, d=d, mq=mq, tc=tc: E.matmul(
                        pj[:, 0:512],
                        y_bf[:, d * L + tc * 128:d * L + (tc + 1) * 128],
                        wo_bf[:, d * DM + mq * 512:d * DM + (mq + 1) * 512],
                        start=(d == 0), stop=(d == DCH - 1)),
                        inc=(pe if d == DCH - 1 else None))
                P.wait("scalar", pe)
                if ost_dma:
                    P.wait("scalar", ld, ost_dma)
                P.run("scalar", lambda E, pj=pj, mq=mq: E.activation(
                    ostage[:, mq * 512:(mq + 1) * 512], pj[:, 0:512], AF.Copy),
                    inc=ac)
                ps_state[bi] = (ac, ac.n)
            P.wait("sync", ac)
            dma("sync", ld, rs_i.rearrange("(c p) m -> c p m", p=128)[tc],
                ostage[:])
            ost_dma = ld.n
        P.wait("gpsimd", ld)
        P.run("gpsimd", lambda E: E.collective_compute(
            "ReduceScatter", OP.add, replica_groups=[[0, 1, 2, 3], [4, 5, 6, 7]],
            ins=[rs_i], outs=[rs_o]), inc=c_rs)
        sB2.close()
        sB.close()

        sC = ExitStack()
        x1_sb = sb_(sC, "x1_sb", [128, 8 * DM])
        s_wnb = sb_(sC, "s_wnb", [128, DM])
        hq_fm = sb_(sC, "hq_fm", [128, 16 * TQ], BF16)
        dma("sync", ld, x1_sb[0:1, 0:DM], wnorm_r)
        ld_wn = ld.n
        for j in range(4):
            bi, pj = ps_get()
            P.wait("tensor", ld, ld_wn)
            ps_wait("tensor", bi)
            P.run("tensor", lambda E, pj=pj, j=j: E.matmul(
                pj[:, 0:512], const_ones[0:1, :],
                x1_sb[0:1, j * 512:(j + 1) * 512], start=True, stop=True), inc=pe)
            P.wait("scalar", pe)
            P.run("scalar", lambda E, pj=pj, j=j: E.activation(
                s_wnb[:, j * 512:(j + 1) * 512], pj[:, 0:512], AF.Copy), inc=ac)
            ps_state[bi] = (ac, ac.n)
        ac_wnb = ac.n

        sC1 = ExitStack()
        f_in = [sb_(sC1, f"f_in{i}", [128, DM]) for i in range(2)]
        f_hid = sb_(sC1, "f_hid", [128, DM])
        h_t = sb_(sC1, "h_t", [128, DM])
        hq_tm = sb_(sC1, "hq_tm", [128, DM], BF16)
        f_state = [None, None]
        hid_rd = None
        hq_dma = 0
        for tc in range(8):
            fb = f_in[tc % 2]
            if f_state[tc % 2]:
                P.wait("gpsimd", *f_state[tc % 2])
            P.wait("gpsimd", c_rs)
            dma("gpsimd", gd, fb[:], rs_o.rearrange("(c p) m -> c p m", p=128)[tc])
            if hid_rd:
                P.wait("sync", *hid_rd)
            dma("sync", ld, f_hid[:], hidQ.rearrange("(c p) m -> c p m", p=128)[tc])
            P.wait("vector", gd)
            P.wait("vector", ld)
            P.run("vector", lambda E, fb=fb: E.tensor_add(fb[:], fb[:], f_hid[:]),
                  inc=ve)
            hid_rd = (ve, ve.n)
            P.run("vector", lambda E, fb=fb: E.tensor_reduce(
                stat[:, 0:1], fb[:], AX.X, OP.add), inc=ve)
            P.wait("scalar", ve)
            P.run("scalar", lambda E: E.activation(
                stat[:, 1:2], stat[:, 0:1], AF.Copy, scale=-1.0 / DM), inc=ac)
            P.wait("vector", ac)
            P.run("vector", lambda E, fb=fb: E.tensor_scalar(
                fb[:], fb[:], stat[:, 1:2], None, OP.add), inc=ve)
            P.wait("scalar", ve)
            P.run("scalar", lambda E, fb=fb: E.activation(
                h_t[:], fb[:], AF.Square, accum_out=stat[:, 2:3]), inc=ac)
            P.wait("vector", ac)
            P.run("vector", lambda E: E.tensor_scalar(
                stat[:, 3:4], stat[:, 2:3], 1.0 / DM, EPS_LN, OP.mult, OP.add),
                inc=ve)
            P.wait("scalar", ve)
            P.run("scalar", lambda E: E.activation(
                stat[:, 4:5], stat[:, 3:4], AF.Sqrt), inc=ac)
            P.wait("vector", ac)
            P.run("vector", lambda E: E.reciprocal(stat[:, 5:6], stat[:, 4:5]),
                  inc=ve)
            P.wait("scalar", ve)
            P.run("scalar", lambda E: E.activation(
                stat[:, 23:24], stat[:, 5:6], AF.Copy), inc=ac)
            P.wait("vector", ac)
            P.run("vector", lambda E, fb=fb, tc=tc: E.tensor_scalar(
                x1_sb[:, tc * DM:(tc + 1) * DM], fb[:], stat[:, 23:24], None,
                OP.mult), inc=ve)
            f_state[tc % 2] = (ve, ve.n)
            P.wait("scalar", ve)
            P.wait("scalar", ac, ac_wnb)
            P.run("scalar", lambda E, tc=tc: E.activation(
                h_t[:], x1_sb[:, tc * DM:(tc + 1) * DM], AF.Square,
                accum_out=stat[:, 6:7]), inc=ac)
            P.wait("vector", ac)
            P.run("vector", lambda E: E.tensor_scalar(
                stat[:, 7:8], stat[:, 6:7], 1.0 / DM, EPS_LN, OP.mult, OP.add),
                inc=ve)
            P.wait("scalar", ve)
            P.run("scalar", lambda E: E.activation(
                stat[:, 8:9], stat[:, 7:8], AF.Sqrt), inc=ac)
            P.wait("vector", ac)
            P.run("vector", lambda E: E.reciprocal(stat[:, 9:10], stat[:, 8:9]),
                  inc=ve)
            P.wait("scalar", ve)
            P.run("scalar", lambda E: E.activation(
                stat[:, 24:25], stat[:, 9:10], AF.Copy), inc=ac)
            P.wait("vector", ac)
            P.run("vector", lambda E, tc=tc: E.scalar_tensor_tensor(
                h_t[:], x1_sb[:, tc * DM:(tc + 1) * DM], stat[:, 24:25], s_wnb[:],
                OP.mult, OP.mult), inc=ve)
            P.run("vector", lambda E: E.tensor_reduce(
                stat[:, 10:11], h_t[:], AX.X, OP.max, apply_absolute_value=True),
                inc=ve)
            P.wait("scalar", ve)
            P.run("scalar", lambda E: E.activation(
                stat[:, 25:26], stat[:, 10:11], AF.Copy), inc=ac)
            P.wait("vector", ac)
            P.run("vector", lambda E, tc=tc: E.tensor_scalar(
                sgv[:, tc:tc + 1], stat[:, 25:26], EPS_Q, sc_sb[:, 5:6],
                OP.max, OP.mult), inc=ve)
            P.run("vector", lambda E: E.tensor_single_scalar(
                stat[:, 11:12], stat[:, 25:26], EPS_Q, OP.max), inc=ve)
            P.wait("scalar", ve)
            P.run("scalar", lambda E, tc=tc: E.activation(
                shinv[:, tc:tc + 1], stat[:, 11:12], AF.Copy, scale=1.0 / 127.0),
                inc=ac)
            P.wait("vector", ac)
            P.run("vector", lambda E, tc=tc: E.reciprocal(
                stat[:, 12:13], shinv[:, tc:tc + 1]), inc=ve)
            P.wait("scalar", ve)
            if hq_dma:
                P.wait("scalar", ld, hq_dma)
            P.run("scalar", lambda E: E.activation(
                h_t[:], h_t[:], AF.Copy, scale=stat[:, 12:13], bias=MAGIC), inc=ac)
            P.run("scalar", lambda E: E.activation(
                hq_tm[:], h_t[:], AF.Copy, bias=-MAGIC), inc=ac)
            P.wait("sync", ac)
            dma("sync", ld, hq_tm_sp.rearrange("(c p) m -> c p m", p=128)[tc],
                hq_tm[:])
            hq_dma = ld.n
        ld_hq = ld.n
        sC1.close()
        P.wait("sync", ld, ld_hq)
        for k in range(16):
            dma_t("sync", ld, hq_fm[:, k * TQ:(k + 1) * TQ],
                  hq_tm_sp[:, k * 128:(k + 1) * 128])
        ld_hqfm = ld.n

        sC2 = ExitStack()
        wgv_t = [sb_(sC2, f"wgv_t{i}", [128, 32 * 512], BF16) for i in range(2)]
        u_st = [sb_(sC2, f"u_st{i}", [128, 512]) for i in range(2)]
        sg_st = [sb_(sC2, f"sg_st{i}", [128, 512]) for i in range(2)]
        wgv_state = [None, None]
        ust_state = [None, None]
        sg_state = [None, None]
        P.wait("gpsimd", ve)
        P.wait("gpsimd", ac)
        P.wait("gpsimd", ld, ld_hqfm)
        for j in range(16):
            wb = wgv_t[j % 2]
            if wgv_state[j % 2]:
                P.wait("sync", *wgv_state[j % 2])
                P.wait("gpsimd", *wgv_state[j % 2])
            P.wait("sync", c_gv)
            P.wait("gpsimd", c_gv)
            for k in range(16):
                dma("sync", ld, wb[:, k * 512:(k + 1) * 512],
                    gvq_o.rearrange("(k p) o -> k p o", p=128)[k][
                        :, j * 512:(j + 1) * 512])
            for k in range(16):
                dma("gpsimd", gd, wb[:, (16 + k) * 512:(17 + k) * 512],
                    gvq_o.rearrange("(k p) o -> k p o", p=128)[k][
                        :, DH + j * 512:DH + (j + 1) * 512])
            ld_w = ld.n
            gd_w = gd.n
            for tc in range(8):
                biV, pjV = ps_get()
                biG, pjG = ps_get()
                P.wait("tensor", ld, max(ld_w, ld_hqfm))
                P.wait("tensor", gd, gd_w)
                ps_wait("tensor", biV)
                ps_wait("tensor", biG)
                for k in range(16):
                    P.run("tensor", lambda E, pjV=pjV, k=k, tc=tc, wb=wb: E.matmul(
                        pjV[:, 0:512],
                        hq_fm[:, k * TQ + tc * 128:k * TQ + (tc + 1) * 128],
                        wb[:, k * 512:(k + 1) * 512],
                        start=(k == 0), stop=(k == 15)),
                        inc=(pe if k == 15 else None))
                for k in range(16):
                    P.run("tensor", lambda E, pjG=pjG, k=k, tc=tc, wb=wb: E.matmul(
                        pjG[:, 0:512],
                        hq_fm[:, k * TQ + tc * 128:k * TQ + (tc + 1) * 128],
                        wb[:, (16 + k) * 512:(17 + k) * 512],
                        start=(k == 0), stop=(k == 15)),
                        inc=(pe if k == 15 else None))
                P.wait("scalar", pe)
                sgb = sg_st[tc % 2]
                if sg_state[tc % 2]:
                    P.wait("scalar", *sg_state[tc % 2])
                P.run("scalar", lambda E, pjG=pjG, tc=tc, sgb=sgb: E.activation(
                    sgb[:], pjG[:, 0:512], AF.Silu, scale=sgv[:, tc:tc + 1]),
                    inc=ac)
                ub2 = u_st[tc % 2]
                if ust_state[tc % 2]:
                    P.wait("vector", *ust_state[tc % 2])
                P.wait("vector", ac)
                P.run("vector", lambda E, pjV=pjV, tc=tc, ub2=ub2, sgb=sgb:
                      E.scalar_tensor_tensor(
                    ub2[:], pjV[:, 0:512], sgv[:, tc:tc + 1], sgb[:],
                    OP.mult, OP.mult), inc=ve)
                sg_state[tc % 2] = (ve, ve.n)
                ps_state[biV] = (ve, ve.n)
                ps_state[biG] = (ve, ve.n)
                P.run("vector", lambda E, ub2=ub2, tc=tc, j=j: E.tensor_reduce(
                    red_all[:, tc * 16 + j:tc * 16 + j + 1], ub2[:], AX.X,
                    OP.max, apply_absolute_value=True), inc=ve)
                P.wait("sync", ve)
                dma("sync", ld,
                    uffn_sp[:, tc * DH + j * 512:tc * DH + (j + 1) * 512], ub2[:])
                ust_state[tc % 2] = (ld, ld.n)
            wgv_state[j % 2] = (pe, pe.n)
        ld_uffn = ld.n
        sC2.close()

        sC3 = ExitStack()
        ub_f = sb_(sC3, "ub_f", [128, DH])
        uq_st = sb_(sC3, "uq_st", [128, DH], BF16)
        ub_state = None
        uq_dma = 0
        for tc in range(8):
            if ub_state:
                P.wait("sync", *ub_state)
            P.wait("sync", ld, ld_uffn)
            dma("sync", ld, ub_f[:], uffn_sp[:, tc * DH:(tc + 1) * DH])
            P.wait("vector", ld)
            P.run("vector", lambda E, tc=tc: E.tensor_reduce(
                stat[:, 14:15], red_all[:, tc * 16:(tc + 1) * 16], AX.X, OP.max),
                inc=ve)
            P.wait("scalar", ve)
            P.run("scalar", lambda E: E.activation(
                stat[:, 28:29], stat[:, 14:15], AF.Copy), inc=ac)
            P.wait("vector", ac)
            P.run("vector", lambda E: E.tensor_single_scalar(
                stat[:, 15:16], stat[:, 28:29], EPS_Q, OP.max), inc=ve)
            P.wait("scalar", ve)
            P.run("scalar", lambda E: E.activation(
                stat[:, 29:30], stat[:, 15:16], AF.Copy, scale=1.0 / 127.0),
                inc=ac)
            P.wait("vector", ac)
            P.run("vector", lambda E, tc=tc: E.tensor_mul(
                suinv[:, tc:tc + 1], stat[:, 29:30], sc_sb[:, 1:2]), inc=ve)
            P.run("vector", lambda E: E.reciprocal(stat[:, 16:17], stat[:, 29:30]),
                  inc=ve)
            P.wait("scalar", ve)
            if uq_dma:
                P.wait("scalar", ld, uq_dma)
            P.run("scalar", lambda E: E.activation(
                ub_f[:], ub_f[:], AF.Copy, scale=stat[:, 16:17], bias=MAGIC),
                inc=ac)
            ub_state = (ac, ac.n)
            P.run("scalar", lambda E: E.activation(
                uq_st[:], ub_f[:], AF.Copy, bias=-MAGIC), inc=ac)
            P.wait("sync", ac)
            dma("sync", ld, uq_tm_sp.rearrange("(c p) m -> c p m", p=128)[tc],
                uq_st[:])
            uq_dma = ld.n
        ld_uq = ld.n
        sC3.close()

        sC4 = ExitStack()
        offn = sb_(sC4, "offn", [128, 8 * DM])
        uq_l = [sb_(sC4, f"uq_l{i}", [128, TQ], BF16) for i in range(2)]
        wo_t = [sb_(sC4, f"wo_t{i}", [128, 512], BF16) for i in range(2)]
        h2_t = sb_(sC4, "h2_t", [128, DM])
        uq_rd = [None, None]
        wo_rd = [None, None]
        P.wait("gpsimd", ve)
        P.wait("gpsimd", ac)
        P.wait("gpsimd", ld, ld_uq)
        for mq in range(4):
            banks = []
            for q in range(8):
                bi, pj = ps_get()
                ps_wait("tensor", bi)
                banks.append((bi, pj))
            for o in range(64):
                uqb = uq_l[o % 2]
                wob = wo_t[o % 2]
                if uq_rd[o % 2]:
                    P.wait("sync", *uq_rd[o % 2])
                    P.wait("gpsimd", *wo_rd[o % 2])
                P.wait("sync", ld, ld_uq)
                P.wait("gpsimd", c_wo)
                dma_t("sync", ld, uqb[:], uq_tm_sp[:, o * 128:(o + 1) * 128])
                dma("gpsimd", gd, wob[:],
                    woq_o.rearrange("(o p) m -> o p m", p=128)[o][
                        :, mq * 512:(mq + 1) * 512])
                P.wait("tensor", ld)
                P.wait("tensor", gd)
                for q in range(8):
                    bi, pj = banks[q]
                    P.run("tensor", lambda E, pj=pj, q=q, o=o, uqb=uqb, wob=wob:
                          E.matmul(
                        pj[:, 0:512], uqb[:, q * 128:(q + 1) * 128], wob[:],
                        start=(o == 0), stop=(o == 63)),
                        inc=(pe if q == 7 else None))
                uq_rd[o % 2] = (pe, pe.n)
                wo_rd[o % 2] = (pe, pe.n)
            P.wait("scalar", pe)
            for q in range(8):
                bi, pj = banks[q]
                P.run("scalar", lambda E, pj=pj, q=q, mq=mq: E.activation(
                    offn[:, q * DM + mq * 512:q * DM + (mq + 1) * 512],
                    pj[:, 0:512], AF.Copy, scale=suinv[:, q:q + 1]), inc=ac)
                ps_state[bi] = (ac, ac.n)
        ac_f4 = ac.n
        for tc in range(8):
            P.wait("vector", ac, ac_f4)
            P.run("vector", lambda E, tc=tc: E.scalar_tensor_tensor(
                offn[:, tc * DM:(tc + 1) * DM], x1_sb[:, tc * DM:(tc + 1) * DM],
                2.0, offn[:, tc * DM:(tc + 1) * DM], OP.mult, OP.add), inc=ve)
            P.run("vector", lambda E, tc=tc: E.tensor_reduce(
                stat[:, 17:18], offn[:, tc * DM:(tc + 1) * DM], AX.X, OP.add),
                inc=ve)
            P.wait("scalar", ve)
            P.run("scalar", lambda E: E.activation(
                stat[:, 18:19], stat[:, 17:18], AF.Copy, scale=-1.0 / DM), inc=ac)
            P.wait("vector", ac)
            P.run("vector", lambda E, tc=tc: E.tensor_scalar(
                offn[:, tc * DM:(tc + 1) * DM], offn[:, tc * DM:(tc + 1) * DM],
                stat[:, 18:19], None, OP.add), inc=ve)
            P.wait("scalar", ve)
            P.run("scalar", lambda E, tc=tc: E.activation(
                h2_t[:], offn[:, tc * DM:(tc + 1) * DM], AF.Square,
                accum_out=stat[:, 19:20]), inc=ac)
            P.wait("vector", ac)
            P.run("vector", lambda E: E.tensor_scalar(
                stat[:, 20:21], stat[:, 19:20], 1.0 / DM, EPS_LN, OP.mult,
                OP.add), inc=ve)
            P.wait("scalar", ve)
            P.run("scalar", lambda E: E.activation(
                stat[:, 21:22], stat[:, 20:21], AF.Sqrt), inc=ac)
            P.wait("vector", ac)
            P.run("vector", lambda E: E.reciprocal(stat[:, 22:23], stat[:, 21:22]),
                  inc=ve)
            P.wait("scalar", ve)
            P.run("scalar", lambda E: E.activation(
                stat[:, 30:31], stat[:, 22:23], AF.Copy), inc=ac)
            P.wait("vector", ac)
            P.run("vector", lambda E, tc=tc: E.tensor_scalar(
                offn[:, tc * DM:(tc + 1) * DM], offn[:, tc * DM:(tc + 1) * DM],
                stat[:, 30:31], None, OP.mult), inc=ve)
            P.wait("sync", ve)
            dma("sync", so, out_q.rearrange("(c p) m -> c p m", p=128)[tc],
                offn[:, tc * DM:(tc + 1) * DM])
        P.wait("sync", so, so.n)
        sC4.close()
        sC.close()

        blk = g.enter_context(nc.Block())

        @blk.sync
        def _(E):
            for f in P.streams["sync"]:
                f(E)

        @blk.vector
        def _(E):
            for f in P.streams["vector"]:
                f(E)

        @blk.scalar
        def _(E):
            for f in P.streams["scalar"]:
                f(E)

        @blk.tensor
        def _(E):
            for f in P.streams["tensor"]:
                f(E)

        @blk.gpsimd
        def _(E):
            for f in P.streams["gpsimd"]:
                f(E)

    return nc


def kernel(L0_attn_hidden_states, L0_attn_in_proj_w, L0_attn_conv_w,
           L0_attn_conv_b, L0_attn_x_proj_w, L0_attn_dt_proj_w,
           L0_attn_dt_proj_b, L0_attn_out_proj_w, L0_attn_negA, L0_attn_D,
           L0_ffn_w_norm, L0_ffn_w_gv, L0_ffn_w_out):
    hs = np.ascontiguousarray(np.asarray(L0_attn_hidden_states, np.float32))
    Wi = np.asarray(L0_attn_in_proj_w, np.float32)
    convw = np.asarray(L0_attn_conv_w, np.float32)[:, 0, :]
    convb = np.asarray(L0_attn_conv_b, np.float32)
    Wx = np.asarray(L0_attn_x_proj_w, np.float32)
    Wdt = np.asarray(L0_attn_dt_proj_w, np.float32)
    bdt = np.asarray(L0_attn_dt_proj_b, np.float32)
    Wo = np.asarray(L0_attn_out_proj_w, np.float32)
    negA = np.asarray(L0_attn_negA, np.float32)
    Dv = np.asarray(L0_attn_D, np.float32)
    wn = np.asarray(L0_ffn_w_norm, np.float32)
    Wgv = np.asarray(L0_ffn_w_gv, np.float32)
    Wout = np.asarray(L0_ffn_w_out, np.float32)

    oh = np.zeros((32, 32 * 128), dtype=bfdt)
    for n in range(32):
        oh[n, n * 128:(n + 1) * 128] = 1
    ones_r = np.ones((1, 128), np.float32)
    ones_c = np.ones((128, 1), np.float32)

    def chfold(a, nch):
        X = a.shape[1]
        return np.ascontiguousarray(
            a.reshape(nch, 128, X).transpose(1, 0, 2).reshape(128, nch * X))

    in_maps = []
    for c in range(8):
        b, r = c // 4, c % 4
        sl = slice(r * 1024, (r + 1) * 1024)
        wInT = np.ascontiguousarray(
            np.concatenate([Wi[sl], Wi[4096 + r * 1024:4096 + (r + 1) * 1024]],
                           0).T)
        m = dict(
            hidT=np.ascontiguousarray(hs[b].T),
            hidQ=np.ascontiguousarray(hs[b, sl, :]),
            wInT=wInT,
            convW=chfold(convw[sl], DCH),
            convB=chfold(convb[sl][:, None], DCH),
            negA_i=chfold(negA[sl], DCH),
            D_i=chfold(Dv[sl][:, None], DCH),
            dtb_i=chfold(bdt[sl][:, None], DCH),
            xpT=chfold(np.ascontiguousarray(Wx[:, sl].T), DCH),
            dtwT=np.ascontiguousarray(Wdt[sl].T),
            woT=chfold(np.ascontiguousarray(Wo[:, sl].T), DCH),
            wnorm_r=np.ascontiguousarray(wn[None, :]),
            ones_r=ones_r,
            ones_c=ones_c,
            oh_bf=oh,
            wgvT_s=np.ascontiguousarray(Wgv.T[c * 256:(c + 1) * 256]),
            woutT_s=np.ascontiguousarray(Wout.T[c * 1024:(c + 1) * 1024]),
        )
        in_maps.append(m)

    nc = build_nc()
    res = run_bass_kernel_spmd(nc, in_maps, core_ids=list(range(8)))
    out = np.empty((B, L, DM), np.float32)
    for c in range(8):
        b, r = c // 4, c % 4
        out[b, r * 1024:(r + 1) * 1024, :] = res.results[c]["out_q"]
    return out


# version 15
# speedup vs baseline: 1.1305x; 1.1305x over previous
import numpy as np
import ml_dtypes
from contextlib import ExitStack
import concourse.bass as bass
import concourse.mybir as mybir
from concourse.bass_utils import run_bass_kernel_spmd

F32 = mybir.dt.float32
BF16 = mybir.dt.bfloat16
AF = mybir.ActivationFunctionType
OP = mybir.AluOpType
AX = mybir.AxisListType
bfdt = ml_dtypes.bfloat16

B, L, DM = 2, 4096, 2048
DI, NST, RK, CW = 4096, 16, 128, 4
DH = 8192
DCH = 8
TQ = 1024
SEG = 1024
NSEG = L // SEG
MAGIC = float(np.float32(1.5 * 2.0 ** 23))
EPS_Q = 1e-5
EPS_LN = 1e-5
ENGINES = ("sync", "vector", "scalar", "tensor", "gpsimd")


class Sem:
    def __init__(self, stack, nc, name):
        self.h = stack.enter_context(nc.semaphore(name))
        self.n = 0


class Prog:

    def __init__(self):
        self.streams = {e: [] for e in ENGINES}

    def wait(self, eng, sem, val=None):
        v = sem.n if val is None else val
        if v <= 0:
            return
        self.streams[eng].append(lambda E, h=sem.h, v=v: E.wait_ge(h, v))

    def run(self, eng, fn, inc=None, inck=1):
        if inc is not None:
            inc.n += inck
            self.streams[eng].append(
                lambda E, fn=fn, h=inc.h, k=inck: fn(E).then_inc(h, k))
        else:
            self.streams[eng].append(fn)


def build_nc():
    nc = bass.Bass("TRN2", target_bir_lowering=False)
    P = Prog()

    def din(name, shape, dt=F32):
        return nc.dram_tensor(name, shape, dt, kind="ExternalInput").ap()

    hidT = din("hidT", [DM, L])
    hidQ = din("hidQ", [TQ, DM])
    wInT = din("wInT", [DM, 2048])
    convW = din("convW", [128, DCH * CW])
    convB = din("convB", [128, DCH])
    negA_i = din("negA_i", [128, DCH * NST])
    D_i = din("D_i", [128, DCH])
    dtb_i = din("dtb_i", [128, DCH])
    xpT = din("xpT", [128, DCH * 160])
    dtwT = din("dtwT", [128, 1024])
    woT = din("woT", [128, DCH * DM])
    wnorm_r = din("wnorm_r", [1, DM])
    ones_r = din("ones_r", [1, 128])
    ones_c = din("ones_c", [128, 1])
    oh_bf = din("oh_bf", [32, 32 * 128], BF16)
    wgvT_s = din("wgvT_s", [256, DH * 2])
    woutT_s = din("woutT_s", [1024, DM])

    out_q = nc.dram_tensor("out_q", [TQ, DM], F32, kind="ExternalOutput").ap()

    z_sp = nc.dram_tensor("z_sp", [128, DCH * L], BF16).ap()
    u_sp = nc.dram_tensor("u_sp", [128, DCH * L], F32).ap()
    xdbl_i = nc.dram_tensor("xdbl_i", [160, L], F32).ap()
    xdbl_o = nc.dram_tensor("xdbl_o", [160, L], F32).ap()
    wst_i = nc.dram_tensor("wst_i", [1, 16], F32).ap()
    wst_o = nc.dram_tensor("wst_o", [1, 16], F32).ap()
    gvq_i = nc.dram_tensor("gvq_i", [256, DH * 2], BF16).ap()
    gvq_o = nc.dram_tensor("gvq_o", [DM, DH * 2], BF16, addr_space="Shared").ap()
    woq_i = nc.dram_tensor("woq_i", [1024, DM], BF16).ap()
    woq_o = nc.dram_tensor("woq_o", [DH, DM], BF16, addr_space="Shared").ap()
    rs_i = nc.dram_tensor("rs_i", [L, DM], F32).ap()
    rs_o = nc.dram_tensor("rs_o", [TQ, DM], F32).ap()
    hq_tm_sp = nc.dram_tensor("hq_tm_sp", [TQ, DM], BF16).ap()
    uffn_sp = nc.dram_tensor("uffn_sp", [128, 8 * DH], F32).ap()
    uq_tm_sp = nc.dram_tensor("uq_tm_sp", [TQ, DH], BF16).ap()

    g = ExitStack()
    sb_ = lambda st, n, s, d=F32: st.enter_context(nc.sbuf_tensor(n, s, d))

    with g:
        ld = Sem(g, nc, "ld"); gd = Sem(g, nc, "gd")
        pe = Sem(g, nc, "pe"); ve = Sem(g, nc, "ve"); ac = Sem(g, nc, "ac")
        c_xd = Sem(g, nc, "c_xd"); c_ws = Sem(g, nc, "c_ws")
        c_gv = Sem(g, nc, "c_gv"); c_wo = Sem(g, nc, "c_wo")
        c_rs = Sem(g, nc, "c_rs"); so = Sem(g, nc, "so")
        po = Sem(g, nc, "po")

        psum = [g.enter_context(nc.psum_tensor(f"ps{i}", [128, 512], F32))
                for i in range(8)]
        ps_state = [None] * 8
        ps_rr = [0]

        def ps_get():
            i = ps_rr[0]; ps_rr[0] = (ps_rr[0] + 1) % 8
            return i, psum[i]

        def ps_wait(eng, bi):
            if ps_state[bi]:
                P.wait(eng, *ps_state[bi])

        const_ones = sb_(g, "const_ones", [1, 128])
        const_onec = sb_(g, "const_onec", [128, 1])
        const_oh = sb_(g, "const_oh", [32, 32 * 128], BF16)
        s_convW = sb_(g, "s_convW", [128, DCH * CW])
        s_convB = sb_(g, "s_convB", [128, DCH])
        s_negA = sb_(g, "s_negA", [128, DCH * NST])
        s_D = sb_(g, "s_D", [128, DCH])
        s_dtb = sb_(g, "s_dtb", [128, DCH])
        stat = sb_(g, "stat", [128, 32])
        carry = sb_(g, "carry", [128, 128])
        red1 = sb_(g, "red1", [128, 1])
        gv_rsum = sb_(g, "gv_rsum", [128, 2])
        wst_sb = sb_(g, "wst_sb", [1, 16])
        sc_sb = sb_(g, "sc_sb", [128, 8])
        shinv = sb_(g, "shinv", [128, 8])
        sgv = sb_(g, "sgv", [128, 8])
        umax = sb_(g, "umax", [128, 8])
        suinv = sb_(g, "suinv", [128, 8])
        red_all = sb_(g, "red_all", [128, 128])
        wq_f = sb_(g, "wq_f", [128, 1024])
        wq_g = sb_(g, "wq_g", [128, 1024])
        wq_b = sb_(g, "wq_b", [128, 1024], BF16)

        def dma(eng, sem, out, in_):
            P.run(eng, lambda E, o=out, i=in_: E.dma_start(out=o, in_=i),
                  inc=sem, inck=16)

        def dma_t(eng, sem, out, in_):
            P.run(eng, lambda E, o=out, i=in_: E.dma_start(out=o, in_=i,
                                                           transpose=True),
                  inc=sem, inck=16)

        dma("sync", ld, const_ones[:], ones_r)
        dma("sync", ld, const_onec[:], ones_c)
        dma("sync", ld, const_oh[:], oh_bf)
        dma("sync", ld, s_convW[:], convW)
        dma("sync", ld, s_convB[:], convB)
        dma("sync", ld, s_negA[:], negA_i)
        dma("sync", ld, s_D[:], D_i)
        dma("sync", ld, s_dtb[:], dtb_i)
        ld_consts = ld.n

        sA = ExitStack()
        x_bf = sb_(sA, "x_bf", [128, DCH * (L + 8)], BF16)
        sA1 = ExitStack()
        wIn_bf = sb_(sA1, "wIn_bf", [128, 16 * 2048], BF16)
        wtmp = sb_(sA1, "wtmp", [128, 1024])
        hid_f = sb_(sA1, "hid_f", [128, 16 * 256])
        hid_b = [sb_(sA1, f"hid_b{i}", [128, 16 * 256], BF16) for i in range(2)]
        zstage = sb_(sA1, "zstage", [128, DCH * 256], BF16)

        for d in range(DCH):
            P.run("vector", lambda E, d=d: E.memset(
                x_bf[:, d * (L + 8):d * (L + 8) + 8], 0.0), inc=ve)
        wcast = []
        for k in range(32):
            if k >= 1:
                P.wait("sync", ve, wcast[k - 1])
            dma("sync", ld, wtmp[:], wInT.rearrange(
                "(k p) e -> k p e", p=128)[k // 2][
                :, (k % 2) * 1024:(k % 2 + 1) * 1024])
            P.wait("vector", ld)
            P.run("vector", lambda E, k=k: E.tensor_copy(
                wIn_bf[:, k * 1024:(k + 1) * 1024], wtmp[:]), inc=ve)
            wcast.append(ve.n)
        ve_win = ve.n

        hidb_state = [None, None]
        hidf_state = None
        z_dma = 0
        for tc in range(16):
            bsel = tc % 2
            if hidf_state:
                P.wait("sync", *hidf_state)
            for kq in range(4):
                dma("sync", ld, hid_f[:, kq * 4 * 256:(kq + 1) * 4 * 256],
                    hidT.rearrange("(k p) t -> p k t", p=128)[
                        :, kq * 4:(kq + 1) * 4, tc * 256:(tc + 1) * 256])
            P.wait("vector", ld)
            if hidb_state[bsel]:
                P.wait("vector", *hidb_state[bsel])
            P.run("vector", lambda E, b=bsel: E.tensor_copy(
                hid_b[b][:], hid_f[:]), inc=ve)
            hidf_state = (ve, ve.n)
            cast_cnt = ve.n
            for m in range(16):
                bi, pj = ps_get()
                pv = pj[:, 0:256]
                P.wait("tensor", ve, max(cast_cnt, ve_win))
                ps_wait("tensor", bi)
                for k in range(16):
                    P.run("tensor", lambda E, pv=pv, m=m, k=k, b=bsel: E.matmul(
                        pv, wIn_bf[:, k * 2048 + m * 128:k * 2048 + (m + 1) * 128],
                        hid_b[b][:, k * 256:(k + 1) * 256],
                        start=(k == 0), stop=(k == 15)),
                        inc=(pe if k == 15 else None))
                P.wait("scalar", pe)
                if m < 8:
                    P.run("scalar", lambda E, pv=pv, m=m, tc=tc: E.activation(
                        x_bf[:, m * (L + 8) + 8 + tc * 256:
                             m * (L + 8) + 8 + (tc + 1) * 256], pv, AF.Copy),
                        inc=ac)
                else:
                    if m == 8 and z_dma:
                        P.wait("scalar", ld, z_dma)
                    P.run("scalar", lambda E, pv=pv, m=m: E.activation(
                        zstage[:, (m - 8) * 256:(m - 7) * 256], pv, AF.Copy),
                        inc=ac)
                ps_state[bi] = (ac, ac.n)
            hidb_state[bsel] = (pe, pe.n)
            P.wait("sync", ac)
            for zc in range(8):
                dma("sync", ld, z_sp[:, zc * L + tc * 256:zc * L + (tc + 1) * 256],
                    zstage[:, zc * 256:(zc + 1) * 256])
            z_dma = ld.n
        ac_inproj = ac.n
        sA1.close()

        P.wait("gpsimd", pe)
        P.wait("gpsimd", ld, z_dma)
        qsrc = []
        for t in range(2):
            for q in range(16):
                qsrc.append((wgvT_s.rearrange("(t p) o -> t p o", p=128)[t][
                    :, q * 1024:(q + 1) * 1024],
                    gvq_i.rearrange("(t p) o -> t p o", p=128)[t][
                    :, q * 1024:(q + 1) * 1024], 1024, 0))
        for t in range(8):
            for q in range(2):
                qsrc.append((woutT_s.rearrange("(t p) o -> t p o", p=128)[t][
                    :, q * 1024:(q + 1) * 1024],
                    woq_i.rearrange("(t p) o -> t p o", p=128)[t][
                    :, q * 1024:(q + 1) * 1024], 1024, 1))
        q1_state = [None, None]
        for i, (srcp, dstp, w, which) in enumerate(qsrc):
            buf = wq_f if i % 2 == 0 else wq_g
            if q1_state[i % 2]:
                P.wait("gpsimd", ac, q1_state[i % 2])
            dma("gpsimd", gd, buf[:, 0:w], srcp)
            P.wait("scalar", gd)
            P.run("scalar", lambda E, i=i, w=w, buf=buf: E.activation(
                wq_b[:, 0:w], buf[:, 0:w], AF.Abs,
                accum_out=red_all[:, i:i + 1]), inc=ac)
            q1_state[i % 2] = ac.n
        last_red = ac.n
        P.wait("scalar", ac, last_red)
        P.run("scalar", lambda E: E.activation(
            red_all[:, 48:96], red_all[:, 0:48], AF.Copy), inc=ac)
        P.wait("vector", ac)
        P.run("vector", lambda E: E.tensor_reduce(
            gv_rsum[:, 0:1], red_all[:, 48:80], AX.X, OP.add), inc=ve)
        P.run("vector", lambda E: E.tensor_reduce(
            gv_rsum[:, 1:2], red_all[:, 80:96], AX.X, OP.add), inc=ve)
        last_red = ve.n
        bi, pj = ps_get()
        ps_wait("tensor", bi)
        P.wait("tensor", ve, last_red)
        P.wait("tensor", ld, ld_consts)
        P.run("tensor", lambda E, pj=pj: E.matmul(
            pj[0:1, 0:2], const_onec[:], gv_rsum[:], start=True, stop=True),
            inc=pe)
        P.wait("scalar", pe)
        P.run("scalar", lambda E, pj=pj: E.activation(
            wst_sb[0:1, 0:2], pj[0:1, 0:2], AF.Copy), inc=ac)
        ps_state[bi] = (ac, ac.n)
        P.wait("gpsimd", ac)
        dma("gpsimd", gd, wst_i, wst_sb[:])
        P.wait("gpsimd", gd)
        P.run("gpsimd", lambda E: E.collective_compute(
            "AllReduce", OP.add, replica_groups=[[0, 1, 2, 3, 4, 5, 6, 7]],
            ins=[wst_i], outs=[wst_o]), inc=c_ws)
        gd_q1 = gd.n
        ve_q1 = ve.n

        sA2 = ExitStack()
        cacc = sb_(sA2, "cacc", [128, L])
        u_f32 = sb_(sA2, "u_f32", [128, L])
        u_bf = sb_(sA2, "u_bf", [128, DCH * L], BF16)
        xp_bf = sb_(sA2, "xp_bf", [128, DCH * 160], BF16)
        xdblB = sb_(sA2, "xdblB", [32, L])
        u_dma_prev = 0
        for d in range(DCH):
            base = d * (L + 8) + 8
            P.wait("vector", ac, ac_inproj)
            P.run("vector", lambda E, d=d, base=base: E.tensor_scalar(
                cacc[:], x_bf[:, base - 3:base - 3 + L],
                s_convW[:, d * CW:d * CW + 1], None, OP.mult), inc=ve)
            for j in range(1, 4):
                P.run("vector", lambda E, d=d, base=base, j=j: E.scalar_tensor_tensor(
                    cacc[:], x_bf[:, base - 3 + j:base - 3 + j + L],
                    s_convW[:, d * CW + j:d * CW + j + 1], cacc[:],
                    OP.mult, OP.add), inc=ve)
            P.wait("scalar", ve)
            if u_dma_prev:
                P.wait("scalar", ld, u_dma_prev)
            P.run("scalar", lambda E, d=d: E.activation(
                u_f32[:], cacc[:], AF.Silu, bias=s_convB[:, d:d + 1]), inc=ac)
            P.wait("vector", ac)
            P.run("vector", lambda E, d=d: E.tensor_copy(
                u_bf[:, d * L:(d + 1) * L], u_f32[:]), inc=ve)
            P.wait("sync", ac)
            dma("sync", ld, u_sp[:, d * L:(d + 1) * L], u_f32[:])
            u_dma_prev = ld.n
        ve_conv = ve.n
        ld_usp = ld.n

        dma("sync", ld, cacc[:, 0:DCH * 160], xpT)
        P.wait("vector", ld)
        P.wait("vector", ve, ve_conv)
        P.run("vector", lambda E: E.tensor_copy(xp_bf[:], cacc[:, 0:DCH * 160]),
              inc=ve)
        ve_xp = ve.n
        xdblA = cacc
        for tc in range(8):
            bi, pj = ps_get()
            bi2, pj2 = ps_get()
            P.wait("tensor", ve, ve_xp)
            ps_wait("tensor", bi)
            ps_wait("tensor", bi2)
            for k in range(DCH):
                P.run("tensor", lambda E, pj=pj, k=k, tc=tc: E.matmul(
                    pj[:, 0:512], xp_bf[:, k * 160:k * 160 + 128],
                    u_bf[:, k * L + tc * 512:k * L + (tc + 1) * 512],
                    start=(k == 0), stop=(k == DCH - 1)),
                    inc=(pe if k == DCH - 1 else None))
                P.run("tensor", lambda E, pj2=pj2, k=k, tc=tc: E.matmul(
                    pj2[0:32, 0:512], xp_bf[:, k * 160 + 128:(k + 1) * 160],
                    u_bf[:, k * L + tc * 512:k * L + (tc + 1) * 512],
                    start=(k == 0), stop=(k == DCH - 1)),
                    inc=(pe if k == DCH - 1 else None))
            P.wait("scalar", pe)
            P.run("scalar", lambda E, pj=pj, tc=tc: E.activation(
                xdblA[:, tc * 512:(tc + 1) * 512], pj[:, 0:512], AF.Copy), inc=ac)
            P.run("scalar", lambda E, pj2=pj2, tc=tc: E.activation(
                xdblB[:, tc * 512:(tc + 1) * 512], pj2[0:32, 0:512], AF.Copy),
                inc=ac)
            ps_state[bi] = (ac, ac.n)
            ps_state[bi2] = (ac, ac.n)
        P.wait("gpsimd", ac)
        dma("gpsimd", gd, xdbl_i[0:128, :], xdblA[:])
        dma("gpsimd", gd, xdbl_i[128:160, :], xdblB[:])
        P.wait("gpsimd", gd)
        P.run("gpsimd", lambda E: E.collective_compute(
            "AllReduce", OP.add, replica_groups=[[0, 1, 2, 3], [4, 5, 6, 7]],
            ins=[xdbl_i], outs=[xdbl_o]), inc=c_xd)

        P.wait("gpsimd", c_ws)
        dma("gpsimd", gd, wst_sb[:], wst_o)
        P.wait("vector", gd)
        P.run("vector", lambda E: E.tensor_scalar(
            wst_sb[0:1, 2:3], wst_sb[0:1, 0:1], 1.0 / (16384.0 * 2048.0), EPS_Q,
            OP.mult, OP.max), inc=ve)
        P.run("vector", lambda E: E.tensor_scalar(
            wst_sb[0:1, 3:4], wst_sb[0:1, 1:2], 1.0 / (8192.0 * 2048.0), EPS_Q,
            OP.mult, OP.max), inc=ve)
        P.wait("scalar", ve)
        P.run("scalar", lambda E: E.activation(
            wst_sb[0:1, 6:8], wst_sb[0:1, 2:4], AF.Copy), inc=ac)
        P.wait("vector", ac)
        P.run("vector", lambda E: E.reciprocal(
            wst_sb[0:1, 4:6], wst_sb[0:1, 6:8]), inc=ve)
        ve_wsc = ve.n
        bi, pj = ps_get()
        ps_wait("tensor", bi)
        P.wait("tensor", ve, ve_wsc)
        P.run("tensor", lambda E, pj=pj: E.matmul(
            pj[:, 0:4], const_ones[0:1, :], wst_sb[0:1, 2:6],
            start=True, stop=True), inc=pe)
        P.wait("scalar", pe)
        P.run("scalar", lambda E, pj=pj: E.activation(
            sc_sb[:, 0:4], pj[:, 0:4], AF.Copy), inc=ac)
        ps_state[bi] = (ac, ac.n)
        ac_wsc = ac.n
        P.wait("vector", ac, ac_wsc)
        P.run("vector", lambda E: E.tensor_single_scalar(
            sc_sb[:, 5:6], sc_sb[:, 0:1], 1.0 / 127.0, OP.mult), inc=ve)
        gd_wq = gd.n
        ve_wq = ve.n
        ac_wq = ac.n

        sA2.close()
        sA.close()


        sB = ExitStack()
        y_bf = sb_(sB, "y_bf", [128, DCH * L], BF16)
        sB1 = ExitStack()
        dt_bf = sb_(sB1, "dt_bf", [128, L], BF16)
        BC_bf = sb_(sB1, "BC_bf", [32, L], BF16)
        dtw_bf = sb_(sB1, "dtw_bf", [128, 1024], BF16)
        delta_all = sb_(sB1, "delta_all", [128, DCH * SEG], BF16)
        du_all = sb_(sB1, "du_all", [128, DCH * SEG], BF16)
        y_acc = sb_(sB1, "y_acc", [128, DCH * SEG])
        Bb_sb = [sb_(sB1, f"Bb{i}", [128, SEG], BF16) for i in range(2)]
        Cb_sb = [sb_(sB1, f"Cb{i}", [128, SEG], BF16) for i in range(2)]
        a_t = [sb_(sB1, f"a_t{i}", [128, SEG]) for i in range(2)]
        b_t = [sb_(sB1, f"b_t{i}", [128, SEG]) for i in range(2)]
        s_t2 = [sb_(sB1, f"s_t2{i}", [128, SEG]) for i in range(2)]
        prod_bf = sb_(sB1, "prod_bf", [128, SEG], BF16)
        useg = sb_(sB1, "useg", [128, SEG])
        zseg = sb_(sB1, "zseg", [128, SEG], BF16)

        P.wait("sync", gd, gd_wq)
        P.wait("sync", c_xd)
        dma("sync", ld, y_acc[:, 0:L],
            xdbl_o[0:128, :])
        dma("sync", ld, y_acc[0:32, SEG * 4:SEG * 4 + L], xdbl_o[128:160, :])
        ld_xdbl = ld.n
        P.wait("vector", ld, ld_xdbl)
        P.wait("vector", ve, ve_wq)
        P.run("vector", lambda E: E.tensor_copy(
            dt_bf[:], y_acc[:, 0:L]), inc=ve)
        P.run("vector", lambda E: E.tensor_copy(
            BC_bf[:], y_acc[0:32, SEG * 4:SEG * 4 + L]), inc=ve)
        dma("sync", ld, useg[:], dtwT)
        P.wait("vector", ld)
        P.run("vector", lambda E: E.tensor_copy(dtw_bf[:], useg[:, 0:1024]),
              inc=ve)
        ve_dtprep = ve.n
        fin_state = (ve, ve_dtprep)

        ab_state = [None, None]
        bc_state = [None, None]
        P.run("vector", lambda E: E.memset(stat[:, 31:32], 0.0), inc=ve)
        for sg in range(NSEG):
            t0 = sg * SEG
            P.run("vector", lambda E: E.memset(y_acc[:], 0.0), inc=ve)
            for d in range(DCH):
                sp0, sp1 = a_t[0], b_t[0]
                bi, pj = ps_get()
                bi2, pj2 = ps_get()
                P.wait("tensor", ve, ve_dtprep)
                if ab_state[0]:
                    P.wait("tensor", *ab_state[0])
                ps_wait("tensor", bi)
                ps_wait("tensor", bi2)
                for half, px in ((0, pj), (1, pj2)):
                    P.run("tensor", lambda E, px=px, d=d, half=half, t0=t0: E.matmul(
                        px[:, 0:512], dtw_bf[:, d * 128:(d + 1) * 128],
                        dt_bf[:, t0 + half * 512:t0 + (half + 1) * 512],
                        start=True, stop=True), inc=pe)
                P.wait("vector", pe)
                P.wait("vector", ac, ac_wq)
                if ab_state[0]:
                    P.wait("vector", *ab_state[0])
                for half, px in ((0, pj), (1, pj2)):
                    P.run("vector", lambda E, px=px, d=d, half=half, sp0=sp0:
                          E.tensor_scalar(
                        sp0[:, half * 512:(half + 1) * 512], px[:, 0:512],
                        s_dtb[:, d:d + 1], None, OP.add), inc=ve)
                ps_state[bi] = (ve, ve.n)
                ps_state[bi2] = (ve, ve.n)
                P.wait("scalar", ve)
                P.run("scalar", lambda E, sp0=sp0, sp1=sp1: E.activation(
                    sp1[:], sp0[:], AF.Abs), inc=ac)
                P.run("scalar", lambda E, sp1=sp1: E.activation(
                    sp1[:], sp1[:], AF.Exp, scale=-1.0), inc=ac)
                P.run("scalar", lambda E, sp1=sp1: E.activation(
                    sp1[:], sp1[:], AF.Ln, bias=1.0), inc=ac)
                P.wait("vector", ac)
                P.run("vector", lambda E, d=d, sp0=sp0, sp1=sp1: E.scalar_tensor_tensor(
                    delta_all[:, d * SEG:(d + 1) * SEG], sp0[:], 0.0, sp1[:],
                    OP.max, OP.add), inc=ve)
                if fin_state:
                    P.wait("sync", *fin_state)
                P.wait("sync", ld, ld_usp)
                dma("sync", ld, useg[:], u_sp[:, d * L + t0:d * L + t0 + SEG])
                P.wait("vector", ld)
                P.run("vector", lambda E, d=d: E.tensor_mul(
                    du_all[:, d * SEG:(d + 1) * SEG],
                    delta_all[:, d * SEG:(d + 1) * SEG], useg[:]), inc=ve)
                fin_state = (ve, ve.n)
                ab_state = [None, None]
            ve_du = ve.n

            for n in range(NST):
                par = n % 2
                for which, dst in ((0, Bb_sb[par]), (1, Cb_sb[par])):
                    ohn = n if which == 0 else 16 + n
                    for h2 in range(2):
                        bi, pj = ps_get()
                        P.wait("tensor", ve, ve_dtprep)
                        if bc_state[par]:
                            P.wait("tensor", *bc_state[par])
                        ps_wait("tensor", bi)
                        P.run("tensor", lambda E, pj=pj, ohn=ohn, h2=h2, t0=t0:
                              E.matmul(
                            pj[:, 0:512], const_oh[:, ohn * 128:(ohn + 1) * 128],
                            BC_bf[:, t0 + h2 * 512:t0 + (h2 + 1) * 512],
                            start=True, stop=True), inc=pe)
                        P.wait("scalar", pe)
                        P.run("scalar", lambda E, pj=pj, dst=dst, h2=h2:
                              E.activation(
                            dst[:, h2 * 512:(h2 + 1) * 512], pj[:, 0:512],
                            AF.Copy), inc=ac)
                        ps_state[bi] = (ac, ac.n)
                ac_bc = ac.n
                for d in range(DCH):
                    abp = (n * DCH + d) % 2
                    at, bt = a_t[abp], b_t[abp]
                    if ab_state[abp]:
                        P.wait("scalar", *ab_state[abp])
                    P.wait("scalar", ve, ve_du)
                    P.run("scalar", lambda E, d=d, n=n, at=at: E.activation(
                        at[:], delta_all[:, d * SEG:(d + 1) * SEG], AF.Exp,
                        scale=s_negA[:, d * NST + n:d * NST + n + 1]), inc=ac)
                    if (n * DCH + d) % 4 == 0:
                        P.wait("vector", ac)
                        P.run("vector", lambda E, d=d, bt=bt, par=par: E.tensor_mul(
                            bt[:], du_all[:, d * SEG:(d + 1) * SEG], Bb_sb[par][:]),
                            inc=ve)
                    else:
                        P.wait("gpsimd", ac, ac_bc)
                        P.wait("gpsimd", ve, ve_du)
                        if ab_state[abp]:
                            P.wait("gpsimd", *ab_state[abp])
                        P.run("gpsimd", lambda E, d=d, bt=bt, par=par: E.tensor_mul(
                            bt[:], du_all[:, d * SEG:(d + 1) * SEG], Bb_sb[par][:]),
                            inc=po)
                        P.wait("vector", po)
                    cidx = d * NST + n
                    stt = s_t2[abp]
                    if sg == 0:
                        P.run("vector", lambda E, at=at, bt=bt, stt=stt:
                              E.tensor_tensor_scan(
                            stt[:], at[:], bt[:], 0.0, OP.mult, OP.add), inc=ve)
                    else:
                        P.run("vector", lambda E, at=at, bt=bt, cidx=cidx, stt=stt:
                              E.tensor_tensor_scan(
                            stt[:], at[:], bt[:], carry[:, cidx:cidx + 1],
                            OP.mult, OP.add), inc=ve)
                    ab_state[abp] = (ve, ve.n)
                    P.run("vector", lambda E, par=par, stt=stt: E.tensor_mul(
                        prod_bf[:], stt[:], Cb_sb[par][:]), inc=ve)
                    P.run("vector", lambda E, d=d: E.tensor_add(
                        y_acc[:, d * SEG:(d + 1) * SEG],
                        y_acc[:, d * SEG:(d + 1) * SEG], prod_bf[:]), inc=ve)
                    if sg < NSEG - 1:
                        P.run("vector", lambda E, cidx=cidx, stt=stt: E.tensor_add(
                            carry[:, cidx:cidx + 1], stt[:, SEG - 1:SEG],
                            stat[:, 31:32]), inc=ve)
                bc_state[par] = (ve, ve.n)
            for d in range(DCH):
                if fin_state:
                    P.wait("sync", *fin_state)
                dma("sync", ld, useg[:], u_sp[:, d * L + t0:d * L + t0 + SEG])
                dma("sync", ld, zseg[:], z_sp[:, d * L + t0:d * L + t0 + SEG])
                P.wait("scalar", ld)
                P.wait("scalar", ve)
                gseg = a_t[0]
                P.run("scalar", lambda E, gseg=gseg: E.activation(
                    gseg[:], zseg[:], AF.Silu), inc=ac)
                P.wait("vector", ld)
                P.wait("vector", ac)
                P.run("vector", lambda E, d=d: E.scalar_tensor_tensor(
                    useg[:], useg[:], s_D[:, d:d + 1],
                    y_acc[:, d * SEG:(d + 1) * SEG], OP.mult, OP.add), inc=ve)
                P.run("vector", lambda E, d=d, t0=t0, gseg=gseg: E.tensor_mul(
                    y_bf[:, d * L + t0:d * L + t0 + SEG], useg[:], gseg[:]),
                    inc=ve)
                fin_state = (ve, ve.n)
                ab_state[0] = (ve, ve.n)
        ve_y = ve.n

        qbuf_state = [(ac, ac.n), (ac, ac.n)]
        for i, (srcp, dstp, w, which) in enumerate(qsrc):
            bsl = i % 2
            scl = sc_sb[:, 2:3] if which == 0 else sc_sb[:, 3:4]
            src_f = wq_f if bsl == 0 else wq_g
            if qbuf_state[bsl]:
                P.wait("sync", *qbuf_state[bsl])
            dma("sync", ld, src_f[:, 0:w], srcp)
            P.wait("scalar", ld)
            P.wait("scalar", ac, ac_wsc)
            P.wait("scalar", ve)
            P.run("scalar", lambda E, w=w, scl=scl, src_f=src_f: E.activation(
                src_f[:, 0:w], src_f[:, 0:w], AF.Copy, scale=scl, bias=MAGIC),
                inc=ac)
            P.run("scalar", lambda E, w=w, src_f=src_f: E.activation(
                src_f[:, 0:w], src_f[:, 0:w], AF.Copy, bias=-MAGIC), inc=ac)
            qbuf_state[bsl] = (ac, ac.n)
            P.wait("vector", ac)
            P.run("vector", lambda E, w=w, src_f=src_f: E.tensor_scalar(
                wq_b[:, 0:w], src_f[:, 0:w], -1.0, 1.0, OP.max, OP.min), inc=ve)
            P.wait("sync", ve)
            dma("sync", ld, dstp, wq_b[:, 0:w])
        ld_qw = ld.n
        P.wait("gpsimd", ld, ld_qw)
        P.run("gpsimd", lambda E: E.collective_compute(
            "AllGather", OP.bypass, replica_groups=[[0, 1, 2, 3, 4, 5, 6, 7]],
            ins=[gvq_i], outs=[gvq_o]), inc=c_gv)
        P.run("gpsimd", lambda E: E.collective_compute(
            "AllGather", OP.bypass, replica_groups=[[0, 1, 2, 3, 4, 5, 6, 7]],
            ins=[woq_i], outs=[woq_o]), inc=c_wo)
        gd_wq2 = gd.n
        sB1.close()

        sB2 = ExitStack()
        wo_bf = sb_(sB2, "wo_bf", [128, DCH * DM], BF16)
        ostage = sb_(sB2, "ostage", [128, DM])
        wtmp2 = sb_(sB2, "wtmp2", [128, DM])
        wo_cast = 0
        for d in range(DCH):
            if wo_cast:
                P.wait("sync", ve, wo_cast)
            dma("sync", ld, wtmp2[:], woT.rearrange("p (d m) -> d p m", d=DCH)[d])
            P.wait("vector", ld)
            P.run("vector", lambda E, d=d: E.tensor_copy(
                wo_bf[:, d * DM:(d + 1) * DM], wtmp2[:]), inc=ve)
            wo_cast = ve.n
        ve_wo = ve.n
        ost_dma = 0
        for tc in range(32):
            for mq in range(4):
                bi, pj = ps_get()
                P.wait("tensor", ve, max(ve_y, ve_wo))
                ps_wait("tensor", bi)
                for d in range(DCH):
                    P.run("tensor", lambda E, pj=pj, d=d, mq=mq, tc=tc: E.matmul(
                        pj[:, 0:512],
                        y_bf[:, d * L + tc * 128:d * L + (tc + 1) * 128],
                        wo_bf[:, d * DM + mq * 512:d * DM + (mq + 1) * 512],
                        start=(d == 0), stop=(d == DCH - 1)),
                        inc=(pe if d == DCH - 1 else None))
                P.wait("scalar", pe)
                if ost_dma:
                    P.wait("scalar", ld, ost_dma)
                P.run("scalar", lambda E, pj=pj, mq=mq: E.activation(
                    ostage[:, mq * 512:(mq + 1) * 512], pj[:, 0:512], AF.Copy),
                    inc=ac)
                ps_state[bi] = (ac, ac.n)
            P.wait("sync", ac)
            dma("sync", ld, rs_i.rearrange("(c p) m -> c p m", p=128)[tc],
                ostage[:])
            ost_dma = ld.n
        P.wait("gpsimd", ld)
        P.run("gpsimd", lambda E: E.collective_compute(
            "ReduceScatter", OP.add, replica_groups=[[0, 1, 2, 3], [4, 5, 6, 7]],
            ins=[rs_i], outs=[rs_o]), inc=c_rs)
        sB2.close()
        sB.close()

        sC = ExitStack()
        x1_sb = sb_(sC, "x1_sb", [128, 8 * DM])
        s_wnb = sb_(sC, "s_wnb", [128, DM])
        hq_fm = sb_(sC, "hq_fm", [128, 16 * TQ], BF16)
        dma("sync", ld, x1_sb[0:1, 0:DM], wnorm_r)
        ld_wn = ld.n
        for j in range(4):
            bi, pj = ps_get()
            P.wait("tensor", ld, ld_wn)
            ps_wait("tensor", bi)
            P.run("tensor", lambda E, pj=pj, j=j: E.matmul(
                pj[:, 0:512], const_ones[0:1, :],
                x1_sb[0:1, j * 512:(j + 1) * 512], start=True, stop=True), inc=pe)
            P.wait("scalar", pe)
            P.run("scalar", lambda E, pj=pj, j=j: E.activation(
                s_wnb[:, j * 512:(j + 1) * 512], pj[:, 0:512], AF.Copy), inc=ac)
            ps_state[bi] = (ac, ac.n)
        ac_wnb = ac.n

        sC1 = ExitStack()
        f_in = [sb_(sC1, f"f_in{i}", [128, DM]) for i in range(2)]
        f_hid = sb_(sC1, "f_hid", [128, DM])
        h_t = sb_(sC1, "h_t", [128, DM])
        hq_tm = sb_(sC1, "hq_tm", [128, DM], BF16)
        f_state = [None, None]
        hid_rd = None
        hq_dma = 0
        for tc in range(8):
            fb = f_in[tc % 2]
            if f_state[tc % 2]:
                P.wait("gpsimd", *f_state[tc % 2])
            P.wait("gpsimd", c_rs)
            dma("gpsimd", gd, fb[:], rs_o.rearrange("(c p) m -> c p m", p=128)[tc])
            if hid_rd:
                P.wait("sync", *hid_rd)
            dma("sync", ld, f_hid[:], hidQ.rearrange("(c p) m -> c p m", p=128)[tc])
            P.wait("vector", gd)
            P.wait("vector", ld)
            P.run("vector", lambda E, fb=fb: E.tensor_add(fb[:], fb[:], f_hid[:]),
                  inc=ve)
            hid_rd = (ve, ve.n)
            P.run("vector", lambda E, fb=fb: E.tensor_reduce(
                stat[:, 0:1], fb[:], AX.X, OP.add), inc=ve)
            P.wait("scalar", ve)
            P.run("scalar", lambda E: E.activation(
                stat[:, 1:2], stat[:, 0:1], AF.Copy, scale=-1.0 / DM), inc=ac)
            P.wait("vector", ac)
            P.run("vector", lambda E, fb=fb: E.tensor_scalar(
                fb[:], fb[:], stat[:, 1:2], None, OP.add), inc=ve)
            P.wait("scalar", ve)
            P.run("scalar", lambda E, fb=fb: E.activation(
                h_t[:], fb[:], AF.Square, accum_out=stat[:, 2:3]), inc=ac)
            P.wait("vector", ac)
            P.run("vector", lambda E: E.tensor_scalar(
                stat[:, 3:4], stat[:, 2:3], 1.0 / DM, EPS_LN, OP.mult, OP.add),
                inc=ve)
            P.wait("scalar", ve)
            P.run("scalar", lambda E: E.activation(
                stat[:, 4:5], stat[:, 3:4], AF.Sqrt), inc=ac)
            P.wait("vector", ac)
            P.run("vector", lambda E: E.reciprocal(stat[:, 5:6], stat[:, 4:5]),
                  inc=ve)
            P.wait("scalar", ve)
            P.run("scalar", lambda E: E.activation(
                stat[:, 23:24], stat[:, 5:6], AF.Copy), inc=ac)
            P.wait("vector", ac)
            P.run("vector", lambda E, fb=fb, tc=tc: E.tensor_scalar(
                x1_sb[:, tc * DM:(tc + 1) * DM], fb[:], stat[:, 23:24], None,
                OP.mult), inc=ve)
            f_state[tc % 2] = (ve, ve.n)
            P.wait("scalar", ve)
            P.wait("scalar", ac, ac_wnb)
            P.run("scalar", lambda E, tc=tc: E.activation(
                h_t[:], x1_sb[:, tc * DM:(tc + 1) * DM], AF.Square,
                accum_out=stat[:, 6:7]), inc=ac)
            P.wait("vector", ac)
            P.run("vector", lambda E: E.tensor_scalar(
                stat[:, 7:8], stat[:, 6:7], 1.0 / DM, EPS_LN, OP.mult, OP.add),
                inc=ve)
            P.wait("scalar", ve)
            P.run("scalar", lambda E: E.activation(
                stat[:, 8:9], stat[:, 7:8], AF.Sqrt), inc=ac)
            P.wait("vector", ac)
            P.run("vector", lambda E: E.reciprocal(stat[:, 9:10], stat[:, 8:9]),
                  inc=ve)
            P.wait("scalar", ve)
            P.run("scalar", lambda E: E.activation(
                stat[:, 24:25], stat[:, 9:10], AF.Copy), inc=ac)
            P.wait("vector", ac)
            P.run("vector", lambda E, tc=tc: E.scalar_tensor_tensor(
                h_t[:], x1_sb[:, tc * DM:(tc + 1) * DM], stat[:, 24:25], s_wnb[:],
                OP.mult, OP.mult), inc=ve)
            P.run("vector", lambda E: E.tensor_reduce(
                stat[:, 10:11], h_t[:], AX.X, OP.max, apply_absolute_value=True),
                inc=ve)
            P.wait("scalar", ve)
            P.run("scalar", lambda E: E.activation(
                stat[:, 25:26], stat[:, 10:11], AF.Copy), inc=ac)
            P.wait("vector", ac)
            P.run("vector", lambda E, tc=tc: E.tensor_scalar(
                sgv[:, tc:tc + 1], stat[:, 25:26], EPS_Q, sc_sb[:, 5:6],
                OP.max, OP.mult), inc=ve)
            P.run("vector", lambda E: E.tensor_single_scalar(
                stat[:, 11:12], stat[:, 25:26], EPS_Q, OP.max), inc=ve)
            P.wait("scalar", ve)
            P.run("scalar", lambda E, tc=tc: E.activation(
                shinv[:, tc:tc + 1], stat[:, 11:12], AF.Copy, scale=1.0 / 127.0),
                inc=ac)
            P.wait("vector", ac)
            P.run("vector", lambda E, tc=tc: E.reciprocal(
                stat[:, 12:13], shinv[:, tc:tc + 1]), inc=ve)
            P.wait("scalar", ve)
            if hq_dma:
                P.wait("scalar", ld, hq_dma)
            P.run("scalar", lambda E: E.activation(
                h_t[:], h_t[:], AF.Copy, scale=stat[:, 12:13], bias=MAGIC), inc=ac)
            P.run("scalar", lambda E: E.activation(
                hq_tm[:], h_t[:], AF.Copy, bias=-MAGIC), inc=ac)
            P.wait("sync", ac)
            dma("sync", ld, hq_tm_sp.rearrange("(c p) m -> c p m", p=128)[tc],
                hq_tm[:])
            hq_dma = ld.n
        ld_hq = ld.n
        sC1.close()
        P.wait("sync", ld, ld_hq)
        for k in range(16):
            dma_t("sync", ld, hq_fm[:, k * TQ:(k + 1) * TQ],
                  hq_tm_sp[:, k * 128:(k + 1) * 128])
        ld_hqfm = ld.n

        sC2 = ExitStack()
        wgv_t = [sb_(sC2, f"wgv_t{i}", [128, 32 * 512], BF16) for i in range(2)]
        u_st = [sb_(sC2, f"u_st{i}", [128, 512]) for i in range(2)]
        sg_st = [sb_(sC2, f"sg_st{i}", [128, 512]) for i in range(2)]
        wgv_state = [None, None]
        ust_state = [None, None]
        sg_state = [None, None]
        P.wait("gpsimd", ve)
        P.wait("gpsimd", ac)
        P.wait("gpsimd", ld, ld_hqfm)
        for j in range(16):
            wb = wgv_t[j % 2]
            if wgv_state[j % 2]:
                P.wait("sync", *wgv_state[j % 2])
                P.wait("gpsimd", *wgv_state[j % 2])
            P.wait("sync", c_gv)
            P.wait("gpsimd", c_gv)
            for k in range(16):
                dma("sync", ld, wb[:, k * 512:(k + 1) * 512],
                    gvq_o.rearrange("(k p) o -> k p o", p=128)[k][
                        :, j * 512:(j + 1) * 512])
            for k in range(16):
                dma("gpsimd", gd, wb[:, (16 + k) * 512:(17 + k) * 512],
                    gvq_o.rearrange("(k p) o -> k p o", p=128)[k][
                        :, DH + j * 512:DH + (j + 1) * 512])
            ld_w = ld.n
            gd_w = gd.n
            for tc in range(8):
                biV, pjV = ps_get()
                biG, pjG = ps_get()
                P.wait("tensor", ld, max(ld_w, ld_hqfm))
                P.wait("tensor", gd, gd_w)
                ps_wait("tensor", biV)
                ps_wait("tensor", biG)
                for k in range(16):
                    P.run("tensor", lambda E, pjV=pjV, k=k, tc=tc, wb=wb: E.matmul(
                        pjV[:, 0:512],
                        hq_fm[:, k * TQ + tc * 128:k * TQ + (tc + 1) * 128],
                        wb[:, k * 512:(k + 1) * 512],
                        start=(k == 0), stop=(k == 15)),
                        inc=(pe if k == 15 else None))
                for k in range(16):
                    P.run("tensor", lambda E, pjG=pjG, k=k, tc=tc, wb=wb: E.matmul(
                        pjG[:, 0:512],
                        hq_fm[:, k * TQ + tc * 128:k * TQ + (tc + 1) * 128],
                        wb[:, (16 + k) * 512:(17 + k) * 512],
                        start=(k == 0), stop=(k == 15)),
                        inc=(pe if k == 15 else None))
                P.wait("scalar", pe)
                sgb = sg_st[tc % 2]
                if sg_state[tc % 2]:
                    P.wait("scalar", *sg_state[tc % 2])
                P.run("scalar", lambda E, pjG=pjG, tc=tc, sgb=sgb: E.activation(
                    sgb[:], pjG[:, 0:512], AF.Silu, scale=sgv[:, tc:tc + 1]),
                    inc=ac)
                ub2 = u_st[tc % 2]
                if ust_state[tc % 2]:
                    P.wait("vector", *ust_state[tc % 2])
                P.wait("vector", ac)
                P.run("vector", lambda E, pjV=pjV, tc=tc, ub2=ub2, sgb=sgb:
                      E.scalar_tensor_tensor(
                    ub2[:], pjV[:, 0:512], sgv[:, tc:tc + 1], sgb[:],
                    OP.mult, OP.mult), inc=ve)
                sg_state[tc % 2] = (ve, ve.n)
                ps_state[biV] = (ve, ve.n)
                ps_state[biG] = (ve, ve.n)
                P.run("vector", lambda E, ub2=ub2, tc=tc, j=j: E.tensor_reduce(
                    red_all[:, tc * 16 + j:tc * 16 + j + 1], ub2[:], AX.X,
                    OP.max, apply_absolute_value=True), inc=ve)
                P.wait("sync", ve)
                dma("sync", ld,
                    uffn_sp[:, tc * DH + j * 512:tc * DH + (j + 1) * 512], ub2[:])
                ust_state[tc % 2] = (ld, ld.n)
            wgv_state[j % 2] = (pe, pe.n)
        ld_uffn = ld.n
        sC2.close()

        sC3 = ExitStack()
        ub_f = sb_(sC3, "ub_f", [128, DH])
        uq_st = sb_(sC3, "uq_st", [128, DH], BF16)
        ub_state = None
        uq_dma = 0
        for tc in range(8):
            if ub_state:
                P.wait("sync", *ub_state)
            P.wait("sync", ld, ld_uffn)
            dma("sync", ld, ub_f[:], uffn_sp[:, tc * DH:(tc + 1) * DH])
            P.wait("vector", ld)
            P.run("vector", lambda E, tc=tc: E.tensor_reduce(
                stat[:, 14:15], red_all[:, tc * 16:(tc + 1) * 16], AX.X, OP.max),
                inc=ve)
            P.wait("scalar", ve)
            P.run("scalar", lambda E: E.activation(
                stat[:, 28:29], stat[:, 14:15], AF.Copy), inc=ac)
            P.wait("vector", ac)
            P.run("vector", lambda E: E.tensor_single_scalar(
                stat[:, 15:16], stat[:, 28:29], EPS_Q, OP.max), inc=ve)
            P.wait("scalar", ve)
            P.run("scalar", lambda E: E.activation(
                stat[:, 29:30], stat[:, 15:16], AF.Copy, scale=1.0 / 127.0),
                inc=ac)
            P.wait("vector", ac)
            P.run("vector", lambda E, tc=tc: E.tensor_mul(
                suinv[:, tc:tc + 1], stat[:, 29:30], sc_sb[:, 1:2]), inc=ve)
            P.run("vector", lambda E: E.reciprocal(stat[:, 16:17], stat[:, 29:30]),
                  inc=ve)
            P.wait("scalar", ve)
            if uq_dma:
                P.wait("scalar", ld, uq_dma)
            P.run("scalar", lambda E: E.activation(
                ub_f[:], ub_f[:], AF.Copy, scale=stat[:, 16:17], bias=MAGIC),
                inc=ac)
            ub_state = (ac, ac.n)
            P.run("scalar", lambda E: E.activation(
                uq_st[:], ub_f[:], AF.Copy, bias=-MAGIC), inc=ac)
            P.wait("sync", ac)
            dma("sync", ld, uq_tm_sp.rearrange("(c p) m -> c p m", p=128)[tc],
                uq_st[:])
            uq_dma = ld.n
        ld_uq = ld.n
        sC3.close()

        sC4 = ExitStack()
        offn = sb_(sC4, "offn", [128, 8 * DM])
        uq_l = [sb_(sC4, f"uq_l{i}", [128, TQ], BF16) for i in range(2)]
        wo_t = [sb_(sC4, f"wo_t{i}", [128, 512], BF16) for i in range(2)]
        h2_t = sb_(sC4, "h2_t", [128, DM])
        uq_rd = [None, None]
        wo_rd = [None, None]
        P.wait("gpsimd", ve)
        P.wait("gpsimd", ac)
        P.wait("gpsimd", ld, ld_uq)
        for mq in range(4):
            banks = []
            for q in range(8):
                bi, pj = ps_get()
                ps_wait("tensor", bi)
                banks.append((bi, pj))
            for o in range(64):
                uqb = uq_l[o % 2]
                wob = wo_t[o % 2]
                if uq_rd[o % 2]:
                    P.wait("sync", *uq_rd[o % 2])
                    P.wait("gpsimd", *wo_rd[o % 2])
                P.wait("sync", ld, ld_uq)
                P.wait("gpsimd", c_wo)
                dma_t("sync", ld, uqb[:], uq_tm_sp[:, o * 128:(o + 1) * 128])
                dma("gpsimd", gd, wob[:],
                    woq_o.rearrange("(o p) m -> o p m", p=128)[o][
                        :, mq * 512:(mq + 1) * 512])
                P.wait("tensor", ld)
                P.wait("tensor", gd)
                for q in range(8):
                    bi, pj = banks[q]
                    P.run("tensor", lambda E, pj=pj, q=q, o=o, uqb=uqb, wob=wob:
                          E.matmul(
                        pj[:, 0:512], uqb[:, q * 128:(q + 1) * 128], wob[:],
                        start=(o == 0), stop=(o == 63)),
                        inc=(pe if q == 7 else None))
                uq_rd[o % 2] = (pe, pe.n)
                wo_rd[o % 2] = (pe, pe.n)
            P.wait("scalar", pe)
            for q in range(8):
                bi, pj = banks[q]
                P.run("scalar", lambda E, pj=pj, q=q, mq=mq: E.activation(
                    offn[:, q * DM + mq * 512:q * DM + (mq + 1) * 512],
                    pj[:, 0:512], AF.Copy, scale=suinv[:, q:q + 1]), inc=ac)
                ps_state[bi] = (ac, ac.n)
        ac_f4 = ac.n
        for tc in range(8):
            P.wait("vector", ac, ac_f4)
            P.run("vector", lambda E, tc=tc: E.scalar_tensor_tensor(
                offn[:, tc * DM:(tc + 1) * DM], x1_sb[:, tc * DM:(tc + 1) * DM],
                2.0, offn[:, tc * DM:(tc + 1) * DM], OP.mult, OP.add), inc=ve)
            P.run("vector", lambda E, tc=tc: E.tensor_reduce(
                stat[:, 17:18], offn[:, tc * DM:(tc + 1) * DM], AX.X, OP.add),
                inc=ve)
            P.wait("scalar", ve)
            P.run("scalar", lambda E: E.activation(
                stat[:, 18:19], stat[:, 17:18], AF.Copy, scale=-1.0 / DM), inc=ac)
            P.wait("vector", ac)
            P.run("vector", lambda E, tc=tc: E.tensor_scalar(
                offn[:, tc * DM:(tc + 1) * DM], offn[:, tc * DM:(tc + 1) * DM],
                stat[:, 18:19], None, OP.add), inc=ve)
            P.wait("scalar", ve)
            P.run("scalar", lambda E, tc=tc: E.activation(
                h2_t[:], offn[:, tc * DM:(tc + 1) * DM], AF.Square,
                accum_out=stat[:, 19:20]), inc=ac)
            P.wait("vector", ac)
            P.run("vector", lambda E: E.tensor_scalar(
                stat[:, 20:21], stat[:, 19:20], 1.0 / DM, EPS_LN, OP.mult,
                OP.add), inc=ve)
            P.wait("scalar", ve)
            P.run("scalar", lambda E: E.activation(
                stat[:, 21:22], stat[:, 20:21], AF.Sqrt), inc=ac)
            P.wait("vector", ac)
            P.run("vector", lambda E: E.reciprocal(stat[:, 22:23], stat[:, 21:22]),
                  inc=ve)
            P.wait("scalar", ve)
            P.run("scalar", lambda E: E.activation(
                stat[:, 30:31], stat[:, 22:23], AF.Copy), inc=ac)
            P.wait("vector", ac)
            P.run("vector", lambda E, tc=tc: E.tensor_scalar(
                offn[:, tc * DM:(tc + 1) * DM], offn[:, tc * DM:(tc + 1) * DM],
                stat[:, 30:31], None, OP.mult), inc=ve)
            P.wait("sync", ve)
            dma("sync", so, out_q.rearrange("(c p) m -> c p m", p=128)[tc],
                offn[:, tc * DM:(tc + 1) * DM])
        P.wait("sync", so, so.n)
        sC4.close()
        sC.close()

        blk = g.enter_context(nc.Block())

        @blk.sync
        def _(E):
            for f in P.streams["sync"]:
                f(E)

        @blk.vector
        def _(E):
            for f in P.streams["vector"]:
                f(E)

        @blk.scalar
        def _(E):
            for f in P.streams["scalar"]:
                f(E)

        @blk.tensor
        def _(E):
            for f in P.streams["tensor"]:
                f(E)

        @blk.gpsimd
        def _(E):
            for f in P.streams["gpsimd"]:
                f(E)

    return nc


def kernel(L0_attn_hidden_states, L0_attn_in_proj_w, L0_attn_conv_w,
           L0_attn_conv_b, L0_attn_x_proj_w, L0_attn_dt_proj_w,
           L0_attn_dt_proj_b, L0_attn_out_proj_w, L0_attn_negA, L0_attn_D,
           L0_ffn_w_norm, L0_ffn_w_gv, L0_ffn_w_out):
    hs = np.ascontiguousarray(np.asarray(L0_attn_hidden_states, np.float32))
    Wi = np.asarray(L0_attn_in_proj_w, np.float32)
    convw = np.asarray(L0_attn_conv_w, np.float32)[:, 0, :]
    convb = np.asarray(L0_attn_conv_b, np.float32)
    Wx = np.asarray(L0_attn_x_proj_w, np.float32)
    Wdt = np.asarray(L0_attn_dt_proj_w, np.float32)
    bdt = np.asarray(L0_attn_dt_proj_b, np.float32)
    Wo = np.asarray(L0_attn_out_proj_w, np.float32)
    negA = np.asarray(L0_attn_negA, np.float32)
    Dv = np.asarray(L0_attn_D, np.float32)
    wn = np.asarray(L0_ffn_w_norm, np.float32)
    Wgv = np.asarray(L0_ffn_w_gv, np.float32)
    Wout = np.asarray(L0_ffn_w_out, np.float32)

    oh = np.zeros((32, 32 * 128), dtype=bfdt)
    for n in range(32):
        oh[n, n * 128:(n + 1) * 128] = 1
    ones_r = np.ones((1, 128), np.float32)
    ones_c = np.ones((128, 1), np.float32)

    def chfold(a, nch):
        X = a.shape[1]
        return np.ascontiguousarray(
            a.reshape(nch, 128, X).transpose(1, 0, 2).reshape(128, nch * X))

    in_maps = []
    for c in range(8):
        b, r = c // 4, c % 4
        sl = slice(r * 1024, (r + 1) * 1024)
        wInT = np.ascontiguousarray(
            np.concatenate([Wi[sl], Wi[4096 + r * 1024:4096 + (r + 1) * 1024]],
                           0).T)
        m = dict(
            hidT=np.ascontiguousarray(hs[b].T),
            hidQ=np.ascontiguousarray(hs[b, sl, :]),
            wInT=wInT,
            convW=chfold(convw[sl], DCH),
            convB=chfold(convb[sl][:, None], DCH),
            negA_i=chfold(negA[sl], DCH),
            D_i=chfold(Dv[sl][:, None], DCH),
            dtb_i=chfold(bdt[sl][:, None], DCH),
            xpT=chfold(np.ascontiguousarray(Wx[:, sl].T), DCH),
            dtwT=np.ascontiguousarray(Wdt[sl].T),
            woT=chfold(np.ascontiguousarray(Wo[:, sl].T), DCH),
            wnorm_r=np.ascontiguousarray(wn[None, :]),
            ones_r=ones_r,
            ones_c=ones_c,
            oh_bf=oh,
            wgvT_s=np.ascontiguousarray(Wgv.T[c * 256:(c + 1) * 256]),
            woutT_s=np.ascontiguousarray(Wout.T[c * 1024:(c + 1) * 1024]),
        )
        in_maps.append(m)

    nc = build_nc()
    res = run_bass_kernel_spmd(nc, in_maps, core_ids=list(range(8)))
    out = np.empty((B, L, DM), np.float32)
    for c in range(8):
        b, r = c // 4, c % 4
        out[b, r * 1024:(r + 1) * 1024, :] = res.results[c]["out_q"]
    return out
